# version 14
# baseline (speedup 1.0000x reference)
import contextlib
import numpy as np
import ml_dtypes
import concourse.bass as bass
import concourse.mybir as mybir
from concourse.bass_utils import run_bass_kernel_spmd
from concourse.bass_core_rust import IndirectOffsetOnAxis

F32 = mybir.dt.float32
BF16 = mybir.dt.bfloat16
I32 = mybir.dt.int32
AF = mybir.ActivationFunctionType
ALU = mybir.AluOpType
AX = mybir.AxisListType
BF = ml_dtypes.bfloat16

NCORE = 8
L, DM, DI, DS, DR, E, FF = 4096, 1024, 2048, 16, 64, 8, 4096
TOKS, CAPL, TC = 512, 192, 256
NCH = L // TC
DTI = DI // 128
FBIG = float(1 << 20)
DEBUG_E = None
DEBUG_MG = None
DEBUG_STOP = None

ENGS = ("v", "s", "t", "g", "y")
PROG = {"V": "v", "S": "s", "T": "t", "G": "g"}


CHAINENG = {"v": "V", "s": "S", "g": "G"}


class Bld:
    def __init__(self):
        self.q = {e: [] for e in ENGS}
        self.cnt = {}
        self.step = {}
        for k in PROG:
            self.cnt[k] = 0
            self.step[k] = 1
        self.chain = set()

    def addsem(self, key, step=16, chained=True):
        self.cnt[key] = 0
        self.step[key] = step
        if chained:
            self.chain.add(key)

    def on(self, eng, fn, waits=(), sig=None, chain=True):
        ev = None
        waits = [w for w in waits if w is not None]
        if sig is not None and sig in self.chain:
            waits.append((sig, self.cnt[sig]))
        pk = CHAINENG.get(eng)
        if pk is not None and (sig is None or sig == pk):
            sig = pk
            if chain:
                waits.append((pk, self.cnt[pk]))
        if sig is not None:
            self.cnt[sig] += self.step[sig]
            ev = (sig, self.cnt[sig])
        self.q[eng].append((tuple(waits), fn, sig))
        return ev

    def now(self, key):
        return (key, self.cnt[key])

    def play(self, name, eng, sems):
        waited = {}
        for waits, fn, sig in self.q[name]:
            mx = {}
            for (k, v) in waits:
                if name == "t" and PROG.get(k) == name:
                    continue
                if v > mx.get(k, 0):
                    mx[k] = v
            for k in mx:
                v = mx[k]
                if waited.get(k, -1) >= v:
                    continue
                eng.wait_ge(sems[k], v)
                waited[k] = v
            inst = fn(eng)
            if sig is not None:
                inst.then_inc(sems[sig], self.step[sig])


def build_program(parts="all"):
    nc = bass.Bass("TRN2", target_bir_lowering=False, detect_race_conditions=False)
    b = Bld()

    def din(name, shape, dt):
        return nc.dram_tensor(name, list(shape), dt, kind="ExternalInput").ap()

    myx = din("myx", (L, DM), F32)
    x0s = din("x0s", (TOKS, DM), F32)
    idxflat = din("idxflat", (1, 2 * TOKS), I32)
    wTd = din("wT", (128, 8), F32)
    flagcol = din("flagcol", (128, 1), F32)
    ag2idx = din("ag2idx", (128, 4), I32)
    ipwT = din("ipwT", (DM, 2 * DI), F32)
    opwT = din("opwT", (DI, DM), F32)
    xpwT = din("xpwT", (DI, 96), F32)
    dtwT = din("dtwT", (DR, DI), F32)
    convwT = din("convwT", (128, DTI * 4), F32)
    convbT = din("convbT", (128, DTI), F32)
    dtbT = din("dtbT", (128, DTI), F32)
    dvecT = din("dvecT", (128, DTI), F32)
    alogT = din("alogT", (128, DTI * DS), F32)
    w1_d = din("w1", (E, DM, FF), F32)
    w2_d = din("w2", (E, FF, DM), F32)
    bias_d = din("bias", (1, DM), F32)
    tokT = din("tokT", (128, 8), F32)
    kofT = din("kofT", (128, 8), F32)
    iota256 = din("iota256", (1, 256), F32)
    eidf = din("eidf", (8, 1), F32)
    identb = din("identb", (128, 128), BF16)
    identf = din("identf", (128, 128), F32)

    out_my = nc.dram_tensor("out_my", [L, DM], F32, kind="ExternalOutput").ap()
    out_b0 = nc.dram_tensor("out_b0", [TOKS, DM], F32, kind="ExternalOutput").ap()

    lnrows_d = nc.dram_tensor("lnrows_d", [TOKS, DM], BF16, kind="Internal").ap()
    outrows_d = nc.dram_tensor("outrows_d", [TOKS + 1, DM], F32, kind="Internal").ap()
    bcB_d = nc.dram_tensor("bcB_d", [16, TC], BF16, kind="Internal").ap()
    bcC_d = nc.dram_tensor("bcC_d", [16, TC], BF16, kind="Internal").ap()
    ag1out = nc.dram_tensor("ag1out", [NCORE * TOKS, DM], F32, kind="Internal",
                            addr_space="Shared").ap()
    ag2in = nc.dram_tensor("ag2in", [L, DM], F32, kind="Internal").ap()
    ag2out = nc.dram_tensor("ag2out", [NCH * NCORE * TC, DM], F32, kind="Internal",
                            addr_space="Shared").ap()
    ipWT_b = nc.dram_tensor("ipWT_b", [DM, 2 * DI], BF16, kind="Internal").ap()
    opWT_b = nc.dram_tensor("opWT_b", [DI, DM], BF16, kind="Internal").ap()

    es = contextlib.ExitStack()
    with es:
        sb = lambda n, sh, dt=F32: es.enter_context(nc.sbuf_tensor(n, list(sh), dt))

        s_identb = sb("c_identb", (128, 128), BF16)
        s_identf = sb("c_identf", (128, 128), F32)
        s_xpw = sb("c_xpw", (128, DTI, 96), BF16)
        s_dtw = sb("c_dtw", (DR, DI), BF16)
        s_convw = sb("c_convw", (128, DTI * 4), F32)
        s_convb = sb("c_convb", (128, DTI), F32)
        s_dtb = sb("c_dtb", (128, DTI), F32)
        s_dvec = sb("c_dvec", (128, DTI), F32)
        s_A2 = sb("c_A2", (128, DTI * DS), F32)
        s_hbias = sb("c_hbias", (128, DM), F32)
        s_flagc = sb("c_flagc", (128, 1), F32)
        s_zeroI = sb("c_zeroI", (128, 1), I32)
        s_hst = sb("c_hst", (128, DTI * DS), F32)
        s_tail = sb("c_tail", (128, DTI * 3), F32)
        s_eid = sb("c_eid", (8, 1), F32)
        s_wT = sb("c_wT", (128, 8), F32)
        s_tok = sb("c_tok", (128, 8), F32)
        s_kof = sb("c_kof", (128, 8), F32)
        s_fbig = sb("c_fbig", (128, 1), F32)
        s_iotaR = sb("c_iotaR", (128, 256), F32)
        s_ag2i = sb("c_ag2i", (128, 4), I32)
        m_rv = sb("c_rv", (128, 16 * 4))
        m_gI = sb("c_gI", (128, 16), I32)
        m_aI = sb("c_aI", (128, 16), I32)
        m_bI = sb("c_bI", (128, 16), I32)

        DMAKEYS = (["CST", "WCA", "HBG", "S1X", "X1L0", "X1L1", "AGL0", "AGL1",
                    "IP0", "IP1", "IP2", "IP3", "OP0", "OP1", "BCD", "BBG", "CBG",
                    "OD0", "OD1", "XD0", "XD1", "FIN", "S3X"] +
                   [t + k for t in ("m1", "m3")
                    for k in ("IDX", "LNR", "PRE", "XG0", "XG1", "SC",
                              "W10", "W11", "W12", "W13",
                              "W20", "W21", "W22", "W23")])
        for k in DMAKEYS:
            b.addsem(k, 16, chained=True)
        b.addsem("CC", 1, chained=True)
        sems = {k: es.enter_context(nc.semaphore("sem_" + k)) for k in b.cnt}
        block = es.enter_context(nc.Block())

        def act(o, i, f, bias=0.0, scale=1.0):
            return lambda e, o=o, i=i, f=f, b_=bias, s_=scale: e.activation(o, i, f, bias=b_, scale=s_)

        def tt(o, i0, i1, op):
            return lambda e, o=o, i0=i0, i1=i1, op=op: e.tensor_tensor(o, i0, i1, op)

        def ts1(o, i0, s1, op0):
            return lambda e, o=o, i0=i0, s1=s1, op0=op0: e.tensor_single_scalar(o, i0, s1, op0)

        def ts2(o, i0, s1, s2, op0, op1):
            return lambda e, o=o, i0=i0, s1=s1, s2=s2, op0=op0, op1=op1: e.tensor_scalar(o, i0, s1, s2, op0, op1)

        def stt(o, i0, s, i1, op0, op1):
            return lambda e, o=o, i0=i0, s=s, i1=i1, op0=op0, op1=op1: e.scalar_tensor_tensor(o, i0, s, i1, op0, op1)

        def cpy(o, i):
            def f(e, o=o, i=i):
                if hasattr(e, "tensor_copy"):
                    return e.tensor_copy(o, i)
                return e.copy(o, i)
            return f

        def sel(o, m, a, z):
            return lambda e, o=o, m=m, a=a, z=z: e.select(o, m, a, z)

        def mm(o, lt, r, st, sp):
            return lambda e, o=o, lt=lt, r=r, st=st, sp=sp: e.matmul(o, lt, r, start=st, stop=sp)

        def trp(o, i, idn):
            return lambda e, o=o, i=i, idn=idn: e.transpose(o, i, idn)

        def dma(o, i):
            return lambda e, o=o, i=i: e.dma_start(o, i)

        def mset(apx, v):
            return lambda e, apx=apx, v=v: e.memset(apx, v)

        def ig(o, i, idxap, bc=None):
            return lambda e, o=o, i=i, idxap=idxap, bc=bc: e.indirect_dma_start(
                o, None, i, IndirectOffsetOnAxis(idxap, 0),
                bounds_check=bc, oob_is_err=False)

        def isca(o, idxap, i, bc):
            return lambda e, o=o, idxap=idxap, i=i, bc=bc: e.indirect_dma_start(
                o, IndirectOffsetOnAxis(idxap, 0), i, None,
                bounds_check=bc, oob_is_err=False, compute_op=ALU.add)

        def rsum(o, i):
            return lambda e, o=o, i=i: e.reduce_sum(o, i, AX.X, None)

        ev_c = []
        for (dst, src) in ((s_identb[:], identb), (s_identf[:], identf),
                           (s_convw[:], convwT), (s_convb[:], convbT),
                           (s_dtb[:], dtbT), (s_dvec[:], dvecT), (s_A2[:], alogT),
                           (s_flagc[:], flagcol), (s_eid[:], eidf), (s_wT[:], wTd),
                           (s_tok[:], tokT), (s_kof[:], kofT), (s_ag2i[:], ag2idx)):
            ev_c.append(b.on("y", dma(dst, src), sig="CST"))
        ev_consts = ev_c[-1]
        ev_ipc = b.on("g", dma(ipWT_b, ipwT), sig="WCA")
        ev_opc = b.on("g", dma(opWT_b, opwT), sig="WCA")
        ev_xpc = b.on("g", lambda e: e.dma_start(
            s_xpw[:, :, :], xpwT.rearrange("(k p) c -> p k c", p=128)), sig="WCA")
        ev_dtc = b.on("g", dma(s_dtw[:], dtwT), sig="WCA")
        ev_a2 = b.on("s", act(s_A2[:], s_A2[:], AF.Exp), waits=[ev_consts], sig="S")
        ev_a2 = b.on("v", ts1(s_A2[:], s_A2[:], -1.0, ALU.mult), waits=[ev_a2], sig="V")
        ev_z = b.on("v", mset(s_zeroI[:], 0), sig="V")
        b.on("v", mset(s_fbig[:], FBIG))
        ev_hb = b.on("g", ig(s_hbias[:], bias_d, s_zeroI[:]), waits=[ev_z], sig="HBG")
        ev_hb = b.on("v", ts1(s_hbias[:], s_hbias[:], 0.5, ALU.mult), waits=[ev_hb], sig="V")
        ev_iotaR = b.on("g", ig(s_iotaR[:], iota256, s_zeroI[:]), waits=[ev_z], sig="HBG")

        def ln_rows(src_ap, out_bf, scratch, col, waits):
            w = list(waits)
            b.on("v", rsum(col[:, 0:1], src_ap), waits=w)
            evq = b.on("s", act(scratch, src_ap, AF.Square), waits=w, sig="S")
            b.on("v", rsum(col[:, 1:2], scratch), waits=[evq])
            b.on("v", ts1(col[:, 0:1], col[:, 0:1], 1.0 / DM, ALU.mult))
            b.on("v", ts1(col[:, 1:2], col[:, 1:2], 1.0 / DM, ALU.mult))
            b.on("v", tt(col[:, 2:3], col[:, 0:1], col[:, 0:1], ALU.mult))
            b.on("v", tt(col[:, 1:2], col[:, 1:2], col[:, 2:3], ALU.subtract))
            ev = b.on("v", ts1(col[:, 1:2], col[:, 1:2], 1e-5, ALU.add), sig="V")
            ev = b.on("s", act(col[:, 3:4], col[:, 1:2], AF.Ln), waits=[ev], sig="S")
            ev = b.on("v", ts1(col[:, 3:4], col[:, 3:4], -0.5, ALU.mult), waits=[ev], sig="V")
            ev = b.on("s", act(col[:, 3:4], col[:, 3:4], AF.Exp), waits=[ev], sig="S")
            ev = b.on("v", ts2(out_bf, src_ap, col[:, 0:1], col[:, 3:4],
                               ALU.subtract, ALU.mult), waits=[ev], sig="V")
            return ev

        st_moe = {}

        def emit_moe(tag, src_tiles, fence, do_routing=True):
            with contextlib.ExitStack() as ar:
                sba = lambda n, sh, dt=F32: ar.enter_context(nc.sbuf_tensor(tag + n, list(sh), dt))
                psa = lambda n, sh, dt=F32: ar.enter_context(nc.psum_tensor(tag + n, list(sh), dt))
                m_idxb = sba("idxb", (8, 2 * TOKS), I32)
                m_idxf = sba("idxf", (8, 2 * TOKS), F32)
                m_mask = sba("mask", (8, 2 * TOKS), F32)
                m_maskI = sba("maskI", (8, 2 * TOKS), I32)
                m_zero8 = sba("zero8", (8, 2 * TOKS), F32)
                m_rank = sba("rank", (8, 2 * TOKS), F32)
                m_nbig8 = sba("nbig8", (8, 2 * TOKS), F32)
                m_rkT = sba("rkT", (128, 64), F32)
                m_vals = sba("vals", (128, 8 * 4), F32)
                m_oh = sba("oh", (128, 2, 128), F32)
                m_gF = sba("gF", (128, 16), F32)
                m_aF = sba("aF", (128, 16), F32)
                m_bF = sba("bF", (128, 16), F32)
                m_zcol = sba("zcol", (128, 1), F32)
                m_dcol = sba("dcol", (128, 1), F32)
                m_mk = sba("mk", (128, 1), F32)
                m_mkI = sba("mkI", (128, 2), I32)
                m_lnsc = sba("lnsc", (128, DM), F32)
                m_lncol = sba("lncol", (128, 8), F32)
                m_lnb = sba("lnb", (128, DM), BF16)
                m_pre = sba("pre", (128, DM), F32)
                m_XG = sba("XG", (128, 2, DM), BF16)
                m_XT = sba("XT", (128, 2, 8 * CAPL), BF16)
                m_hT = sba("hT", (128, 2, 32 * CAPL), BF16)
                m_w1s = sba("w1s", (128, 4, 512), BF16)
                m_w2s = sba("w2s", (128, 4, 512), BF16)
                m_yT = sba("yT", (128, 8 * CAPL), F32)
                m_yr = sba("yr", (128, 2, DM), F32)
                m_yrW = sba("yrW", (128, 2, DM), F32)
                p_rt = psa("prt", (128, 128), F32)
                p_g1t = [psa("pg1%d" % i, (128, 128), BF16) for i in range(2)]
                p_w1t = [psa("pw1%d" % i, (128, 2 * CAPL), F32) for i in range(2)]
                p_w1 = [p_w1t[i // 2][:, (i % 2) * CAPL:(i % 2 + 1) * CAPL] for i in range(4)]
                p_w2t = [psa("pw2%d" % i, (128, 2 * CAPL), F32) for i in range(2)]
                p_w2 = [p_w2t[i // 2][:, (i % 2) * CAPL:(i % 2 + 1) * CAPL] for i in range(4)]
                p_yt = psa("pyt", (128, 512), F32)

                if do_routing:
                    ev_idx = b.on("g", ig(m_idxb[:], idxflat, s_zeroI[0:8, :]),
                                  waits=[ev_z] + fence, sig=tag + "IDX")
                    ev_idx = b.on("v", cpy(m_idxf[:], m_idxb[:]), waits=[ev_idx] + fence, sig="V")
                    b.on("v", mset(m_zero8[:], 0.0), waits=fence)
                    b.on("v", mset(m_nbig8[:], -FBIG))
                    b.on("v", ts1(m_mask[:], m_idxf[:], s_eid[:, 0:1], ALU.is_equal),
                         waits=[ev_consts, ev_idx])
                    b.on("v", cpy(m_maskI[:], m_mask[:]))
                    b.on("v", lambda e: e.tensor_tensor_scan(
                        m_rank[:], m_mask[:], m_zero8[:], 0.0, ALU.add, ALU.add))
                    ev_rs = b.on("v", sel(m_idxf[:], m_maskI[:], m_rank[:], m_nbig8[:]), sig="V")
                    ev_tr_prev = None
                    evs_rk = []
                    for jt in range(8):
                        wt = [ev_rs] + ([ev_tr_prev] if ev_tr_prev else []) + fence
                        evt = b.on("t", trp(p_rt[:, 0:8], m_idxf[:, jt * 128:(jt + 1) * 128],
                                            s_identf[0:8, 0:8]), waits=wt, sig="T")
                        ev_tr_prev = b.on("s", cpy(m_rkT[:, jt * 8:(jt + 1) * 8], p_rt[:, 0:8]),
                                          waits=[evt] + fence, sig="S")
                        evs_rk.append(ev_tr_prev)
                    ev_rkm1 = b.on("v", ts1(m_rkT[:], m_rkT[:], 1.0, ALU.subtract),
                                   waits=evs_rk, sig="V")
                    ev_vals = []
                    for jt in range(8):
                        c0 = jt * 4
                        b.on("v", cpy(m_vals[:, c0:c0 + 1], s_tok[:, jt:jt + 1]),
                             waits=[ev_consts] + fence)
                        b.on("v", ts1(m_vals[:, c0 + 1:c0 + 2], s_wT[:, jt:jt + 1], 0.5, ALU.mult))
                        b.on("v", cpy(m_vals[:, c0 + 2:c0 + 3], s_kof[:, jt:jt + 1]))
                        ev_vals.append(b.on("v", mset(m_vals[:, c0 + 3:c0 + 4], 1.0), sig="V"))
                    ev_rv = []
                    ev_mm_jt = {}
                    for col in range(16):
                        ee, rt = col // 2, col % 2
                        for jt in range(8):
                            woh = [ev_rkm1, ev_iotaR] + ev_vals
                            key = (col, jt - 2) if jt >= 2 else (col - 1, 6 + jt)
                            if key in ev_mm_jt:
                                woh.append(ev_mm_jt[key])
                            evo = b.on("v", ts1(m_oh[:, jt % 2, :],
                                                s_iotaR[:, rt * 128:(rt + 1) * 128],
                                                m_rkT[:, jt * 8 + ee:jt * 8 + ee + 1],
                                                ALU.is_equal), waits=woh, sig="V")
                            wmm = [evo]
                            if jt == 0 and col > 0:
                                wmm.append(ev_rv[col - 1])
                            ev_mm_jt[(col, jt)] = b.on(
                                "t", mm(p_rt[:, 8:12], m_oh[:, jt % 2, :],
                                        m_vals[:, jt * 4:(jt + 1) * 4], jt == 0, jt == 7),
                                waits=wmm, sig="T")
                        ev_rv.append(b.on("s", cpy(m_rv[:, col * 4:(col + 1) * 4], p_rt[:, 8:12]),
                                          waits=[ev_mm_jt[(col, 7)]], sig="S"))
                    ev_route = []
                    b.on("v", mset(m_zcol[:], 0.0), waits=fence)
                    b.on("v", mset(m_dcol[:], float(TOKS)))
                    for col in range(16):
                        ctok = m_rv[:, col * 4:col * 4 + 1]
                        ckp = m_rv[:, col * 4 + 2:col * 4 + 3]
                        cnt = m_rv[:, col * 4 + 3:col * 4 + 4]
                        w = [ev_rv[col]]
                        b.on("v", cpy(m_mkI[:, 0:1], cnt), waits=w)
                        b.on("v", sel(m_gF[:, col:col + 1], m_mkI[:, 0:1], ctok, m_zcol[:, 0:1]))
                        b.on("v", ts1(m_mk[:, 0:1], ckp, 0.5, ALU.is_lt), waits=w)
                        b.on("v", tt(m_mk[:, 0:1], m_mk[:, 0:1], cnt, ALU.mult))
                        b.on("v", cpy(m_mkI[:, 1:2], m_mk[:, 0:1]))
                        b.on("v", sel(m_aF[:, col:col + 1], m_mkI[:, 1:2], ctok, m_dcol[:, 0:1]))
                        b.on("v", ts1(m_mk[:, 0:1], ckp, 0.5, ALU.is_gt), waits=w)
                        b.on("v", tt(m_mk[:, 0:1], m_mk[:, 0:1], cnt, ALU.mult))
                        b.on("v", cpy(m_mkI[:, 1:2], m_mk[:, 0:1]))
                        ev_route.append(b.on("v", sel(m_bF[:, col:col + 1], m_mkI[:, 1:2], ctok,
                                                      m_dcol[:, 0:1]), sig="V"))
                    b.on("v", cpy(m_gI[:], m_gF[:]), waits=ev_route)
                    b.on("v", cpy(m_aI[:], m_aF[:]))
                    ev_gI = b.on("v", cpy(m_bI[:], m_bF[:]), sig="V")

                    st_moe["ev_rv"] = ev_rv[-1]
                    st_moe["ev_gI"] = ev_gI
                    ev_rv_last = ev_rv[-1]
                else:
                    ev_rv_last = st_moe["ev_rv"]
                    ev_gI = st_moe["ev_gI"]
                ev_lnd = []
                ev_pre = []
                ev_lnw = None
                for rt4 in range(4):
                    apx, rdy = src_tiles[rt4]
                    w = [rdy] + fence + ([ev_lnw] if ev_lnw else [])
                    evr = ln_rows(apx, m_lnb[:], m_lnsc[:], m_lncol[:], w)
                    ev_lnw = b.on("y", dma(lnrows_d[rt4 * 128:(rt4 + 1) * 128, :], m_lnb[:]),
                                  waits=[evr], sig=tag + "LNR")
                    ev_lnd.append(ev_lnw)
                    evp = b.on("v", tt(m_pre[:], apx, s_hbias[:], ALU.add),
                               waits=[rdy, ev_hb] + ([ev_pre[-1]] if ev_pre else []), sig="V")
                    ev_pre.append(b.on("y", dma(outrows_d[rt4 * 128:(rt4 + 1) * 128, :],
                                                m_pre[:]), waits=[evp], sig=tag + "PRE"))
                    if rt4 == 3:
                        ev_pre.append(b.on("y", dma(outrows_d[TOKS:TOKS + 1, :],
                                                    m_pre[0:1, :]),
                                           waits=[evp], sig=tag + "PRE"))

                if DEBUG_STOP == "pre":
                    return ev_pre[-1]
                ev_sc_last = None
                ev_w1done = {0: None, 1: None}
                w1ring_ev = [None] * 4
                w2ring_ev = [None] * 4
                ev_yr_used = {0: None, 1: None}
                ev_yt_last = None
                ev_yrp_roll = []
                gath_tail = []
                for ee in range(DEBUG_E or E):
                    pe = ee % 2
                    ev_gath = list(gath_tail)
                    for rt in range(2):
                        col = ee * 2 + rt
                        wm = ([ev_gI] + ([ev_w1done[pe]] if ev_w1done[pe] else []) +
                              ([ev_gath[-1]] if ev_gath else []) + fence)
                        evg = b.on("g", ig(m_XG[:, pe, :], lnrows_d,
                                           m_gI[:, col:col + 1]),
                                   waits=wm + ev_lnd, sig=tag + "XG" + str(pe))
                        for kb in range(8):
                            evt = b.on("t", trp(p_g1t[kb % 2][:],
                                                m_XG[:, pe, kb * 128:(kb + 1) * 128],
                                                s_identb[:]),
                                       waits=[evg] + ([ev_gath[-2]] if len(ev_gath) >= 2
                                                      else []), sig="T")
                            w_ = 128 if rt == 0 else CAPL - 128
                            ev_gath.append(b.on("s", cpy(
                                m_XT[:, pe, kb * CAPL + rt * 128:
                                     kb * CAPL + rt * 128 + w_],
                                p_g1t[kb % 2][:, 0:w_]),
                                waits=[evt], sig="S"))
                    gath_tail = ev_gath[-2:]
                    ev_h = []
                    nw1 = (DEBUG_MG or 8) * 8
                    w1ld = {}

                    def w1load(j, extra=()):
                        ring = j % 4
                        mg_, k_ = j // 8, j % 8
                        w1ld[j] = b.on("g", dma(
                            m_w1s[:, ring, :],
                            w1_d[ee, k_ * 128:(k_ + 1) * 128, mg_ * 512:(mg_ + 1) * 512]),
                            waits=list(extra) +
                                  ([w1ring_ev[ring]] if w1ring_ev[ring] else []),
                            sig=tag + "W1" + str(ring))
                    for j in range(min(4, nw1)):
                        w1load(j)
                    for mg in range(DEBUG_MG or 8):
                        for k in range(8):
                            j = mg * 8 + k
                            ring = j % 4
                            for ml in range(4):
                                ev_mm = b.on("t", mm(
                                    p_w1[ml], m_w1s[:, ring, ml * 128:(ml + 1) * 128],
                                    m_XT[:, pe, k * CAPL:(k + 1) * CAPL],
                                    k == 0 and ml % 2 == 0, k == 7 and ml % 2 == 1),
                                    waits=[w1ld[j]] + (ev_gath if k == 0 else []) +
                                          (ev_h[-4:] if (k == 0 and ev_h) else []),
                                    sig="T" if ml == 3 else None)
                            w1ring_ev[ring] = ev_mm
                            if j + 4 < nw1:
                                w1load(j + 4)
                        for ml in range(4):
                            mt = mg * 4 + ml
                            ev_h.append(b.on("s", act(m_hT[:, pe, mt * CAPL:(mt + 1) * CAPL],
                                                      p_w1[ml], AF.Gelu),
                                             waits=[ev_mm], sig="S"))
                    ev_w1done[pe] = ev_h[-1]
                    if DEBUG_MG:
                        return ev_h[-1]
                    ev_ye = []
                    w2ld = {}

                    def w2load(j):
                        ring = j % 4
                        mg_, k_ = j // 32, j % 32
                        w2ld[j] = b.on("g", dma(
                            m_w2s[:, ring, :],
                            w2_d[ee, k_ * 128:(k_ + 1) * 128, mg_ * 512:(mg_ + 1) * 512]),
                            waits=[w2ring_ev[ring]] if w2ring_ev[ring] else [],
                            sig=tag + "W2" + str(ring))
                    for j in range(4):
                        w2load(j)
                    for mg in range(2):
                        for k in range(32):
                            j = mg * 32 + k
                            ring = j % 4
                            for ml in range(4):
                                ev_mm = b.on("t", mm(
                                    p_w2[ml], m_w2s[:, ring, ml * 128:(ml + 1) * 128],
                                    m_hT[:, pe, k * CAPL:(k + 1) * CAPL],
                                    k == 0 and ml % 2 == 0, k == 31 and ml % 2 == 1),
                                    waits=[w2ld[j]] + (ev_h if k == 0 else []) +
                                          (ev_ye[-4:] if (k == 0 and ev_ye) else []),
                                    sig="T" if ml == 3 else None)
                            w2ring_ev[ring] = ev_mm
                            if j + 4 < 64:
                                w2load(j + 4)
                        for ml in range(4):
                            mt = mg * 4 + ml
                            ev_ye.append(b.on("s", act(m_yT[:, mt * CAPL:(mt + 1) * CAPL],
                                                       p_w2[ml], AF.Identity),
                                              waits=[ev_mm] +
                                                    ([ev_yt_last] if ev_yt_last else []),
                                              sig="S"))
                    for rt in range(2):
                        col = ee * 2 + rt
                        ev_yrp = []
                        w_ = 128 if rt == 0 else CAPL - 128
                        for ml in range(8):
                            pbuf = (p_yt[0:w_, 0:128] if ml % 2 == 0
                                    else p_rt[0:w_, :])
                            evt = b.on("t", trp(pbuf,
                                                m_yT[:, ml * CAPL + rt * 128:
                                                     ml * CAPL + rt * 128 + w_],
                                                s_identf[:]),
                                       waits=ev_ye + [ev_rv_last] +
                                             ([ev_yr_used[rt]] if ev_yr_used[rt] else []) +
                                             ([ev_yrp_roll[-2]] if len(ev_yrp_roll) >= 2
                                              else []),
                                       sig="T")
                            ev_yt_last = evt
                            ev = b.on("s", cpy(
                                m_yr[0:w_, rt, ml * 128:(ml + 1) * 128],
                                pbuf),
                                waits=[evt], sig="S")
                            ev_yrp.append(ev)
                            ev_yrp_roll.append(ev)
                        cw = m_rv[:, col * 4 + 1:col * 4 + 2]
                        evsA = b.on("v", ts1(m_yrW[0:w_, 0, :], m_yr[0:w_, rt, :],
                                             cw[0:w_, :], ALU.mult),
                                    waits=ev_yrp + [ev_gI] +
                                          ([ev_sc_last] if ev_sc_last else []), sig="V")
                        if w_ < 128:
                            evsA = b.on("v", mset(m_yrW[64:128, 0, :], 0.0),
                                        waits=[ev_sc_last] if ev_sc_last else [],
                                        sig="V")
                        wA = [evsA] + ev_pre + ([ev_sc_last] if ev_sc_last else [])
                        evA = b.on("g", isca(outrows_d, m_aI[:, col:col + 1],
                                             m_yrW[:, 0, :], None), waits=wA, sig=tag + "SC")
                        ev_sc_last = b.on("g", isca(outrows_d, m_bI[:, col:col + 1],
                                                    m_yrW[:, 0, :], None),
                                          waits=[evA], sig=tag + "SC")
                        ev_yr_used[rt] = ev_sc_last
                return ev_sc_last

        with contextlib.ExitStack() as s1s:
            s1_x0 = s1s.enter_context(nc.sbuf_tensor("s1_x0", [128, 4, DM], F32))
            s1_tiles = []
            for rt4 in range(4):
                ev = b.on("y", dma(s1_x0[:, rt4, :], x0s[rt4 * 128:(rt4 + 1) * 128, :]),
                          sig="S1X")
                s1_tiles.append((s1_x0[:, rt4, :], ev))
            ev_moe1 = emit_moe("m1", s1_tiles, [])
            if DEBUG_STOP is None and not (DEBUG_E or DEBUG_MG):
                ev_ag1 = b.on("g", lambda e: e.collective_compute(
                    "AllGather", ALU.bypass, replica_groups=[list(range(NCORE))],
                    ins=[outrows_d[0:TOKS, :].opt()], outs=[ag1out.opt()]),
                    waits=[ev_moe1], sig="CC")

        if parts == "s1":
            block.vector(lambda e: b.play("v", e, sems))
            block.scalar(lambda e: b.play("s", e, sems))
            block.tensor(lambda e: b.play("t", e, sems))
            block.gpsimd(lambda e: b.play("g", e, sems))
            block.sync(lambda e: b.play("y", e, sems))
            es.pop_all().close()
            return nc

        fence1 = ([b.now("V"), b.now("S"), b.now("T"), b.now("G"), b.now("S1X")] +
                  [b.now("m1" + k) for k in ("IDX", "LNR", "PRE", "XG0", "XG1", "SC",
                                             "W10", "W11", "W12", "W13",
                                             "W20", "W21", "W22", "W23")])

        with contextlib.ExitStack() as ma:
            sba = lambda n, sh, dt=F32: ma.enter_context(nc.sbuf_tensor(n, list(sh), dt))
            psa = lambda n, sh, dt=F32: ma.enter_context(nc.psum_tensor(n, list(sh), dt))
            a_x1 = sba("a_x1", (128, 2, 2, DM), F32)
            a_lnsc = sba("a_lnsc", (128, DM), F32)
            a_lncol = sba("a_lncol", (128, 8), F32)
            a_lnb = sba("a_lnb", (128, DM), BF16)
            a_lnT = sba("a_lnT", (128, 8, TC), BF16)
            a_ipw = sba("a_ipw", (128, 4, 512), BF16)
            a_xse = sba("a_xse", (128, DTI * (TC + 3)), BF16)
            a_cs = sba("a_cs", (128, 2, TC), F32)
            a_ub = sba("a_ub", (128, DTI * TC), BF16)
            a_uD = sba("a_uD", (128, 2, DTI * TC), BF16)
            a_g = sba("a_g", (128, 2, DTI * TC), BF16)
            a_dtT = sba("a_dtT", (DR, TC), BF16)
            a_bc = sba("a_bc", (32, TC), BF16)
            a_sp = sba("a_sp", (128, TC), F32)
            a_dl = sba("a_dl", (128, 2, DTI * TC), BF16)
            a_du = sba("a_du", (128, 2, DTI * TC), BF16)
            a_Bb = sba("a_Bb", (128, DS * TC), BF16)
            a_Cb = sba("a_Cb", (128, DS * TC), BF16)
            a_dA = sba("a_dA", (128, 4, 4 * TC), BF16)
            a_dBu = sba("a_dBu", (128, 2, 4 * TC), BF16)
            a_h = sba("a_h", (128, 2, 4 * TC), BF16)
            a_hC = sba("a_hC", (128, 2, 8 * TC), BF16)
            a_yp = sba("a_yp", (128, 2, TC), F32)
            a_yp2 = sba("a_yp2", (128, 2, TC), F32)
            a_r1 = sba("a_r1", (128, 2, 4 * TC), BF16)
            a_r2 = sba("a_r2", (128, 2, 2 * TC), BF16)
            a_yf = sba("a_yf", (128, 2, DTI * TC), BF16)
            a_opw = sba("a_opw", (128, 2, 2, 128), BF16)
            a_yo = sba("a_yo", (128, 2, 2 * TC), F32)
            a_x2q = sba("a_x2q", (128, 2, 256), F32)
            a_outq = sba("a_outq", (128, 2, 256), F32)
            p_lnt = [psa("p_ln%d" % i, (128, 128), BF16) for i in range(2)]
            p_ipt = [psa("p_ip%d" % i, (128, 2 * TC), F32) for i in range(2)]
            p_ip = [p_ipt[i // 2][:, (i % 2) * TC:(i % 2 + 1) * TC] for i in range(4)]
            p_xd = psa("p_xd", (128, 2 * TC), F32)[:, 0:TC]
            p_opt = psa("p_op", (128, 2 * TC), F32)
            p_op = [p_opt[:, 0:TC], p_opt[:, TC:2 * TC]]
            p_yot = [psa("p_yo%d" % i, (128, 128), F32) for i in range(2)]

            st = {}
            st["SDv"] = {}; st["SDg"] = {}; st["TDt"] = {}; st["TDv"] = {}
            st["TDs"] = {}; st["DMAq"] = {0: None, 1: None}
            st["lnw"] = None; st["lnT_used"] = None
            st["ipring"] = [None] * 4
            st["opring"] = [None, None]
            st["dA_guard"] = [None] * 4
            st["hs_guard"] = [None, None]
            st["dB_guard"] = [None, None]
            st["hC_chain"] = [None] * 2
            st["r1_used"] = [None, None]
            st["yp_used"] = [None, None]
            st["yo_used"] = [None, None]
            st["bc_used"] = None
            st["Cb_prev"] = None
            st["blend_prev"] = {0: None, 1: None}
            st["rt_roll"] = []
            st["res_roll"] = []
            st["pdv_prev"] = None
            st["dl_last_prev"] = None
            st["lastip"] = None
            st["lnsc_free"] = None
            st["cs_used"] = [None, None]
            st["xp_done"] = None
            st["x1ev"] = {}
            st["uev"] = {}
            st["gev"] = {}
            st["dlev"] = {}
            st["duev"] = {}
            st["Bbev"] = {}
            st["Cbev"] = {}
            st["yfev"] = {}
            st["x2dma"] = {0: None, 1: None}
            st["outdma"] = {0: None, 1: None}

            def make_head_pieces(c):
                gen = c % 2
                x1ev = {}
                st["x1ev"][c] = x1ev

                def piece_rt(rt):
                    wprev = fence1 if c < 2 else [st["SDv"][c - 2], st["SDg"][c - 2],
                                                  st["TDt"][c - 2], st["TDv"][c - 2],
                                                  st["TDs"][c - 2]]
                    r0 = c * TC + rt * 128
                    evx = b.on("y", dma(a_x1[:, gen, rt, :], myx[r0:r0 + 128, :]),
                               waits=wprev, sig="X1L" + str(rt))
                    eva = b.on("y", dma(a_lnsc[:], ag1out[r0:r0 + 128, :]),
                               waits=[ev_ag1] + ([st["lnsc_free"]] if st["lnsc_free"]
                                                 else []), sig="AGL" + str(rt))
                    ev1 = b.on("v", tt(a_x1[:, gen, rt, :], a_x1[:, gen, rt, :],
                                       s_hbias[:], ALU.add),
                               waits=[evx, ev_hb] + wprev, sig="V")
                    ev1 = b.on("v", tt(a_lnsc[:], a_lnsc[:], a_x1[:, gen, rt, :],
                                       ALU.subtract), waits=[ev1, eva], sig="V")
                    ev1 = b.on("v", stt(a_x1[:, gen, rt, :], a_lnsc[:], s_flagc[:, 0:1],
                                        a_x1[:, gen, rt, :], ALU.mult, ALU.add),
                               waits=[ev1, ev_consts], sig="V")
                    x1ev[rt] = ev1
                    st["blend_prev"][rt] = ev1
                    wln = [ev1] + ([st["lnw"]] if st["lnw"] else [])
                    evl = ln_rows(a_x1[:, gen, rt, :], a_lnb[:], a_lnsc[:], a_lncol[:], wln)
                    st["lnsc_free"] = b.now("V")
                    for kb in range(8):
                        evt = b.on("t", trp(p_lnt[kb % 2][:],
                                            a_lnb[:, kb * 128:(kb + 1) * 128], s_identb[:]),
                                   waits=[evl] + ([st["rt_roll"][-2]]
                                                  if len(st["rt_roll"]) >= 2 else []),
                                   sig="T")
                        st["rt_roll"].append(b.on("s", cpy(
                            a_lnT[:, kb, rt * 128:rt * 128 + 128],
                            p_lnt[kb % 2][:]),
                            waits=[evt] + ([st["lnT_used"]] if st["lnT_used"] else []),
                            sig="S"))
                    st["lnw"] = st["rt_roll"][-1]

                return [lambda rt=rt: piece_rt(rt) for rt in range(2)]

            def make_prod_pieces(c):
                gen = c % 2

                def wpf():
                    return fence1 if c < 2 else [st["SDv"][c - 2], st["SDg"][c - 2],
                                                 st["TDt"][c - 2], st["TDv"][c - 2],
                                                 st["TDs"][c - 2]]
                uev = {}
                gev = {}
                dlev = {}
                duev = {}
                st["uev"][c] = uev
                st["gev"][c] = gev
                st["dlev"][c] = dlev
                st["duev"][c] = duev
                ev_ipev = []
                ipld = {}
                box = {}

                def ipload(eg_, k_):
                    ring = k_ % 4
                    ipld[(eg_, k_)] = b.on("y", dma(
                        a_ipw[:, ring, :],
                        ipWT_b[k_ * 128:(k_ + 1) * 128, eg_ * 512:(eg_ + 1) * 512]),
                        waits=[ev_ipc] + ([st["ipring"][ring]] if st["ipring"][ring]
                                          else []), sig="IP" + str(ring))

                def piece_eg(eg):
                    if eg == 0:
                        for j0 in range(4):
                            ipload(0, j0)
                    ev_rt = st["rt_roll"][-16:]
                    for k in range(8):
                        ring = k % 4
                        for ml in range(4):
                            ev_mm = b.on("t", mm(
                                p_ip[ml], a_ipw[:, ring, ml * 128:(ml + 1) * 128],
                                a_lnT[:, k, :],
                                k == 0 and ml % 2 == 0, k == 7 and ml % 2 == 1),
                                waits=[ipld[(eg, k)]] + (ev_rt if k == 0 else []) +
                                      (ev_ipev[-4:] if (k == 0 and eg > 0) else []) +
                                      ([st["lastip"]] if (k == 0 and eg == 0 and
                                                          st["lastip"]) else []),
                                sig="T" if ml == 3 else None)
                        st["ipring"][ring] = ev_mm
                        nj = eg * 8 + k + 4
                        if nj < 64:
                            ipload(nj // 8, nj % 8)
                    evt = ev_mm
                    for ml in range(4):
                        dt = eg * 4 + ml
                        if dt < DTI:
                            x0 = dt * (TC + 3)
                            u0 = dt * TC
                            cp = dt % 2
                            evc = b.on("s", cpy(a_xse[:, x0 + 3:x0 + 3 + TC], p_ip[ml]),
                                       waits=[evt] + ([st["pdv_prev"]] if st["pdv_prev"]
                                                      else []), sig="S")
                            ev_ipev.append(evc)
                            if c == 0:
                                evtl = b.on("v", mset(a_xse[:, x0:x0 + 3], 0.0),
                                            waits=[st["pdv_prev"]] if st["pdv_prev"]
                                            else [], sig="V")
                            else:
                                evtl = b.on("v", cpy(a_xse[:, x0:x0 + 3],
                                                     s_tail[:, dt * 3:dt * 3 + 3]),
                                            sig="V")
                            b.on("v", cpy(s_tail[:, dt * 3:dt * 3 + 3],
                                          a_xse[:, x0 + TC:x0 + TC + 3]), waits=[evc])
                            evfc = b.on("s", act(a_cs[:, cp, :],
                                                 a_xse[:, x0 + 3:x0 + 3 + TC],
                                                 AF.Identity,
                                                 scale=s_convw[:, dt * 4 + 3:
                                                               dt * 4 + 4]),
                                        waits=[evc, evtl] + ([st["cs_used"][cp]]
                                                             if st["cs_used"][cp]
                                                             else []), sig="S")
                            first = [evfc]
                            for j in (2, 1):
                                b.on("v", stt(a_cs[:, cp, :], a_xse[:, x0 + j:x0 + j + TC],
                                              s_convw[:, dt * 4 + j:dt * 4 + j + 1],
                                              a_cs[:, cp, :], ALU.mult, ALU.add),
                                     waits=first)
                                first = []
                            evcv = b.on("v", stt(a_cs[:, cp, :], a_xse[:, x0:x0 + TC],
                                                 s_convw[:, dt * 4:dt * 4 + 1],
                                                 a_cs[:, cp, :], ALU.mult, ALU.add),
                                        sig="V")
                            evsl = b.on("s", act(a_ub[:, u0:u0 + TC], a_cs[:, cp, :],
                                                 AF.Silu, bias=s_convb[:, dt:dt + 1]),
                                        waits=[evcv, ev_consts] +
                                              ([st["xp_done"]] if st["xp_done"] else []),
                                        sig="S")
                            st["cs_used"][cp] = evsl
                            evud = b.on("s", act(a_uD[:, gen, u0:u0 + TC],
                                                 a_ub[:, u0:u0 + TC], AF.Identity,
                                                 scale=s_dvec[:, dt:dt + 1]),
                                        waits=[evsl] + wpf(), sig="S")
                            uev[dt] = (evsl, evud)
                        else:
                            dr = dt - DTI
                            u0 = dr * TC
                            gev[dr] = b.on("s", act(a_g[:, gen, u0:u0 + TC], p_ip[ml],
                                                    AF.Silu), waits=[evt] + wpf(), sig="S")
                            ev_ipev.append(gev[dr])
                    if eg == 7:
                        st["lastip"] = b.now("S")
                        st["lnT_used"] = b.now("T")

                def piece_xp():
                    for k in range(DTI):
                        ev_mm = b.on("t", mm(p_xd[0:96], s_xpw[:, k, :],
                                             a_ub[:, k * TC:(k + 1) * TC],
                                             k == 0, k == DTI - 1),
                                     waits=[ev_xpc, uev[k][0]] +
                                           ([st["bc_used"]] if (k == 0 and st["bc_used"])
                                            else []) +
                                           ([st["dl_last_prev"]] if (k == 0 and
                                                                     st["dl_last_prev"])
                                            else []),
                                     sig="T" if k == DTI - 1 else None)
                    evt = ev_mm
                    st["xp_done"] = evt
                    ev_dtT = b.on("s", cpy(a_dtT[:, :], p_xd[0:64]), waits=[evt], sig="S")
                    ev_bcv = b.on("s", cpy(a_bc[:, :], p_xd[64:96]), waits=[evt], sig="S")
                    wbc = [ev_bcv] + ([st["Cb_prev"]] if st["Cb_prev"] else [])
                    b.on("y", dma(bcB_d, a_bc[0:16, :]), waits=wbc, sig="BCD")
                    ev_bcd = b.on("y", dma(bcC_d, a_bc[16:32, :]), sig="BCD")
                    st["bc_used"] = ev_bcd
                    box["bcd"] = ev_bcd
                    box["dtT"] = ev_dtT
                    box["bcv"] = ev_bcv

                def piece_dt(q):
                    prev_dl = box.get("prev_dl")
                    for mt in range(q * 4, q * 4 + 4):
                        evmm = b.on("t", mm(p_xd, s_dtw[:, mt * 128:(mt + 1) * 128],
                                            a_dtT[:, :], True, True),
                                    waits=[ev_dtc, box["dtT"], box["bcv"]] +
                                          ([prev_dl] if prev_dl else []), sig="T")
                        d0 = mt * TC
                        evsp = b.on("s", act(a_sp[:, :], p_xd, AF.Exp,
                                             bias=s_dtb[:, mt:mt + 1]),
                                    waits=[evmm, ev_consts], sig="S")
                        dlev[mt] = b.on("s", act(a_dl[:, gen, d0:d0 + TC], a_sp[:, :],
                                                 AF.Ln, bias=1.0),
                                        waits=[evsp] + wpf(), sig="S")
                        prev_dl = dlev[mt]
                        duev[mt] = b.on("v", tt(a_du[:, gen, d0:d0 + TC],
                                                a_dl[:, gen, d0:d0 + TC],
                                                a_ub[:, d0:d0 + TC], ALU.mult),
                                        waits=[dlev[mt]] + wpf(), sig="V",
                                        chain=False)
                    box["prev_dl"] = prev_dl
                    if q == 3:
                        st["dl_last_prev"] = dlev[DTI - 1]
                        st["pdv_prev"] = b.now("V")

                def piece_gather():
                    st["Bbev"][c] = b.on("g", ig(a_Bb[:, :], bcB_d, s_zeroI[:]),
                                         waits=[box["bcd"], ev_z], sig="BBG")
                    st["Cbev"][c] = b.on("g", ig(a_Cb[:, :], bcC_d, s_zeroI[:]),
                                         waits=[box["bcd"], b.now("V")], sig="CBG")
                    st["Cb_prev"] = st["Cbev"][c]

                return ([lambda eg=eg: piece_eg(eg) for eg in range(8)] +
                        [piece_xp] +
                        [lambda q=q: piece_dt(q) for q in range(4)] +
                        [None, None, piece_gather])

            def emit_scan(c, pieces):
                gen = c % 2
                dlev = st["dlev"][c]
                duev = st["duev"][c]
                uev = st["uev"][c]
                gev = st["gev"][c]
                wtd = [st["TDt"][c - 2]] if c >= 2 else []
                yfev = []

                def stage_a(dt):
                    d0 = dt * TC
                    halves = []
                    for half in range(2):
                        hp = (dt * 2 + half) % 2
                        whc = [st["hC_chain"][hp]] if st["hC_chain"][hp] else []
                        hc_evs = []
                        for gl in range(2):
                            grp = half * 2 + gl
                            pg = grp % 2
                            n0 = grp * 4
                            ev_dA = None
                            for nl in range(4):
                                n = n0 + nl
                                ev_dA = b.on("s", act(
                                    a_dA[:, grp, nl * TC:(nl + 1) * TC],
                                    a_dl[:, gen, d0:d0 + TC], AF.Exp,
                                    scale=s_A2[:, dt * DS + n:dt * DS + n + 1]),
                                    waits=[dlev[dt], ev_a2] +
                                          ([st["dA_guard"][grp]]
                                           if st["dA_guard"][grp] else []),
                                    sig="S" if nl == 3 else None)
                            ev_dB = b.on("g", lambda e, pg=pg, gen=gen, d0=d0, n0=n0:
                                         e.tensor_tensor(
                                             a_dBu[:, pg, :].rearrange(
                                                 "p (n t) -> p n t", n=4),
                                             a_du[:, gen, d0:d0 + TC].rearrange(
                                                 "p (o t) -> p o t", o=1).broadcast_to(
                                                 (128, 4, TC)),
                                             a_Bb[:, n0 * TC:(n0 + 4) * TC].rearrange(
                                                 "p (n t) -> p n t", n=4), ALU.mult),
                                         waits=[duev[dt], st["Bbev"][c]] +
                                               ([st["dB_guard"][pg]]
                                                if st["dB_guard"][pg] else []),
                                         sig="G", chain=False)
                            ev_sc = None
                            for nl in range(4):
                                n = n0 + nl
                                ini = (0.0 if c == 0 else
                                       s_hst[:, dt * DS + n:dt * DS + n + 1])
                                ev_sc = b.on("v", lambda e, grp=grp, pg=pg, nl=nl,
                                             ini=ini:
                                             e.tensor_tensor_scan(
                                                 a_h[:, pg, nl * TC:(nl + 1) * TC],
                                                 a_dA[:, grp, nl * TC:(nl + 1) * TC],
                                                 a_dBu[:, pg, nl * TC:(nl + 1) * TC],
                                                 ini, ALU.mult, ALU.add),
                                             waits=[ev_dA, ev_dB] +
                                                   ([st["hs_guard"][pg]]
                                                    if st["hs_guard"][pg] else []),
                                             sig="V", chain=False)
                            st["dA_guard"][grp] = ev_sc
                            st["dB_guard"][pg] = ev_sc
                            st["hs_guard"][pg] = b.on(
                                "s", lambda e, pg=pg, dt=dt, n0=n0: e.copy(
                                    s_hst[:, dt * DS + n0:dt * DS + n0 + 4],
                                    a_h[:, pg, :].rearrange("p (n t) -> p n t", t=TC)
                                    [:, :, TC - 1:TC].rearrange("p n one -> p (n one)")),
                                waits=[ev_sc], sig="S")
                            evhc = b.on("v", lambda e, pg=pg, hp=hp, gl=gl, n0=n0:
                                        e.tensor_tensor(
                                            a_hC[:, hp, gl * 4 * TC:(gl + 1) * 4 * TC],
                                            a_h[:, pg, :],
                                            a_Cb[:, n0 * TC:(n0 + 4) * TC], ALU.mult),
                                        waits=[st["Cbev"][c]] + whc, sig="V")
                            hc_evs.append(evhc)
                        halves.append((hp, hc_evs))
                    return halves

                def stage_l1(dt, halves):
                    l1evs = []
                    for half in range(2):
                        hp, hc_evs = halves[half]
                        hp2 = (dt * 2 + half) % 2
                        ev_l1 = b.on("v", tt(a_r1[:, hp2, :],
                                             a_hC[:, hp, 0:4 * TC],
                                             a_hC[:, hp, 4 * TC:8 * TC], ALU.add),
                                     waits=hc_evs + ([st["r1_used"][hp2]]
                                                     if st["r1_used"][hp2] else []),
                                     sig="V")
                        st["hC_chain"][hp] = ev_l1
                        l1evs.append(ev_l1)
                    return l1evs

                def stage_tree(dt, l1evs):
                    d0 = dt * TC
                    dp = dt % 2
                    ev_red = []
                    for half in range(2):
                        hp2 = (dt * 2 + half) % 2
                        ypbuf = a_yp if half == 0 else a_yp2
                        ev_l2 = b.on("v", tt(a_r2[:, hp2, :],
                                             a_r1[:, hp2, 0:2 * TC],
                                             a_r1[:, hp2, 2 * TC:4 * TC], ALU.add),
                                     waits=[l1evs[half]], sig="V")
                        st["r1_used"][hp2] = ev_l2
                        evrd = b.on("v", tt(ypbuf[:, dp, :],
                                            a_r2[:, hp2, 0:TC],
                                            a_r2[:, hp2, TC:2 * TC], ALU.add),
                                    waits=[st["yp_used"][dp]]
                                          if (half == 0 and st["yp_used"][dp])
                                          else [], sig="V")
                        ev_red.append(evrd)
                    evy = b.on("v", tt(a_yp[:, dp, :], a_yp[:, dp, :],
                                       a_yp2[:, dp, :], ALU.add),
                               waits=ev_red, sig="V")
                    evy = b.on("v", tt(a_yp[:, dp, :], a_yp[:, dp, :],
                                       a_uD[:, gen, d0:d0 + TC], ALU.add),
                               waits=[uev[dt][1]], sig="V")
                    evy = b.on("v", tt(a_yf[:, gen, d0:d0 + TC], a_yp[:, dp, :],
                                       a_g[:, gen, d0:d0 + TC], ALU.mult),
                               waits=[gev[dt]] + wtd, sig="V")
                    st["yp_used"][dp] = evy
                    yfev.append(evy)

                prev = None
                for dt in range(DTI):
                    halves = stage_a(dt)
                    if prev is not None:
                        pdt, phalves = prev
                        l1evs = stage_l1(pdt, phalves)
                        stage_tree(pdt, l1evs)
                    prev = (dt, halves)
                    for p in pieces[dt]:
                        p()
                pdt, phalves = prev
                l1evs = stage_l1(pdt, phalves)
                stage_tree(pdt, l1evs)
                st["SDv"][c] = yfev[-1]
                st["SDg"][c] = b.now("G")
                st["yfev"][c] = yfev

            def make_tail_pieces(c):
                gen = c % 2
                yfev = st["yfev"][c]
                x1ev = st["x1ev"][c]
                evs_out = []
                opld = {}

                def opload(hf_, k_):
                    ring = k_ % 2
                    opld[(hf_, k_)] = b.on("y", dma(
                        a_opw[:, ring, :, :].rearrange("p a b -> p (a b)"),
                        opWT_b[k_ * 128:(k_ + 1) * 128, hf_ * 256:(hf_ + 1) * 256]),
                        waits=[ev_opc] + ([st["opring"][ring]] if st["opring"][ring]
                                          else []), sig="OP" + str(ring))

                def piece_hf(hf):
                    if hf == 0:
                        opload(0, 0)
                        opload(0, 1)
                    prev_oev = st.get("oev_prev", [])
                    yp_ = hf % 2
                    ev_oev = []
                    for k in range(DTI):
                        ring = k % 2
                        for ml in range(2):
                            ev_mm = b.on("t", mm(
                                p_op[ml], a_opw[:, ring, ml, :],
                                a_yf[:, gen, k * TC:(k + 1) * TC],
                                k == 0 and ml == 0, k == DTI - 1 and ml == 1),
                                waits=[opld[(hf, k)]] + (yfev if k == 0 else []) +
                                      (prev_oev if k == 0 else []),
                                sig="T" if ml == 1 else None)
                        st["opring"][ring] = ev_mm
                        nj = hf * DTI + k + 2
                        if nj < 4 * DTI:
                            opload(nj // DTI, nj % DTI)
                    evt = ev_mm
                    for ml in range(2):
                        ev_oev.append(b.on("s", act(a_yo[:, yp_, ml * TC:(ml + 1) * TC],
                                                    p_op[ml], AF.Identity),
                                           waits=[evt] + ([st["yo_used"][yp_]]
                                                          if st["yo_used"][yp_] else []),
                                           sig="S"))
                    for rt in range(2):
                        r0 = c * TC + rt * 128
                        qp = (hf * 2 + rt) % 2
                        ev_tr = []
                        for dq in range(2):
                            evtr = b.on("t", trp(p_yot[dq][:],
                                                 a_yo[:, yp_, dq * TC + rt * 128:
                                                      dq * TC + rt * 128 + 128],
                                                 s_identf[:]),
                                        waits=ev_oev + ([st["res_roll"][-2]]
                                                        if len(st["res_roll"]) >= 2
                                                        else []), sig="T")
                            evr = b.on("v", tt(a_x2q[:, qp, dq * 128:(dq + 1) * 128],
                                               p_yot[dq][:],
                                               a_x1[:, gen, rt,
                                                    hf * 256 + dq * 128:
                                                    hf * 256 + (dq + 1) * 128], ALU.add),
                                       waits=[evtr, x1ev[rt]] +
                                             ([st["x2dma"][qp]] if st["x2dma"][qp]
                                              else []), sig="V", chain=False)
                            ev_tr.append(evr)
                            st["res_roll"].append(evr)
                        evo = b.on("v", tt(a_outq[:, qp, :], a_x2q[:, qp, :],
                                           s_hbias[:, hf * 256:(hf + 1) * 256], ALU.add),
                               waits=ev_tr + ([st["outdma"][qp]] if st["outdma"][qp]
                                              else []), sig="V")
                        evd1 = b.on("y", dma(out_my[r0:r0 + 128,
                                                    hf * 256:(hf + 1) * 256],
                                             a_outq[:, qp, :]), waits=[evo], sig="OD" + str(qp))
                        st["outdma"][qp] = evd1
                        evd2 = b.on("y", dma(ag2in[r0:r0 + 128, hf * 256:(hf + 1) * 256],
                                             a_x2q[:, qp, :]), waits=[ev_tr[-1]],
                                    sig="XD" + str(qp))
                        st["x2dma"][qp] = evd2
                        evs_out.extend([evd1, evd2])
                    st["yo_used"][yp_] = b.now("T")
                    st["oev_prev"] = ev_oev
                    if hf == 3:
                        st["TDt"][c] = b.now("T")
                        st["TDv"][c] = b.now("V")
                        st["TDs"][c] = b.now("S")
                        b.on("g", lambda e, c=c: e.collective_compute(
                            "AllGather", ALU.bypass,
                            replica_groups=[list(range(NCORE))],
                            ins=[ag2in[c * TC:(c + 1) * TC, :].opt()],
                            outs=[ag2out[c * NCORE * TC:
                                         (c + 1) * NCORE * TC, :].opt()]),
                            waits=evs_out, sig="CC")
                        st["lastout"] = evs_out[-1]

                return [lambda hf=hf: piece_hf(hf) for hf in range(4)]

            PRODW = ((6, 0), (7, 1), (8, 2), (9, 3), (10, 8), (11, 4), (11, 9),
                     (12, 5), (12, 10), (13, 6), (13, 11), (14, 7), (14, 12),
                     (15, 15))
            for p in make_head_pieces(0):
                p()
            pp0 = make_prod_pieces(0)
            for _, j in PRODW:
                if pp0[j] is not None:
                    pp0[j]()
            for c in range(NCH):
                slots = [[] for _ in range(DTI)]
                if c >= 1:
                    tps = make_tail_pieces(c - 1)
                    for i in range(4):
                        slots[i].append(tps[i])
                if c + 1 < NCH:
                    hps = make_head_pieces(c + 1)
                    slots[4].append(hps[0])
                    slots[5].append(hps[1])
                    pps = make_prod_pieces(c + 1)
                    for w, j in PRODW:
                        if pps[j] is not None:
                            slots[w].append(pps[j])
                emit_scan(c, slots)
            for p in make_tail_pieces(NCH - 1):
                p()

            fence2 = ([b.now("V"), b.now("S"), b.now("T"), b.now("G"), b.now("CC"),
                       st["lastout"]] +
                      [b.now(k) for k in ("X1L0", "X1L1", "AGL0", "AGL1", "IP0", "IP1",
                                          "OP0", "OP1", "BCD", "BBG", "CBG",
                                          "OD0", "OD1", "XD0", "XD1")])

        with contextlib.ExitStack() as s3s:
            s3_x2 = s3s.enter_context(nc.sbuf_tensor("s3_x2", [128, 4, DM], F32))
            s3_tiles = []
            for rt4 in range(4):
                ev = b.on("g", ig(s3_x2[:, rt4, :], ag2out, s_ag2i[:, rt4:rt4 + 1]),
                          waits=fence2 + [ev_consts], sig="S3X")
                s3_tiles.append((s3_x2[:, rt4, :], ev))
            ev_moe3 = emit_moe("m3", s3_tiles, fence2, do_routing=False)
            ev_fin = b.on("y", dma(out_b0, outrows_d[0:TOKS, :]), waits=[ev_moe3], sig="FIN")
            b.on("y", lambda e: e.nop(), waits=[ev_fin, b.now("OD0"), b.now("OD1"), b.now("m3SC")])

        block.vector(lambda e: b.play("v", e, sems))
        block.scalar(lambda e: b.play("s", e, sems))
        block.tensor(lambda e: b.play("t", e, sems))
        block.gpsimd(lambda e: b.play("g", e, sems))
        block.sync(lambda e: b.play("y", e, sems))

    return nc


_CACHE = {}


def kernel(**inputs):
    x = np.ascontiguousarray(inputs["L0_attn_x"], dtype=np.float32)
    ei = np.ascontiguousarray(inputs["L0_ffn_expert_indices"], dtype=np.int32)
    ew = np.ascontiguousarray(inputs["L0_ffn_expert_weights"], dtype=np.float32)
    _ = inputs.get("L0_ffn_scores")

    if "nc" not in _CACHE:
        _CACHE["nc"] = build_program()
    nc = _CACHE["nc"]

    rep = {
        "ipwT": np.ascontiguousarray(inputs["L0_attn_in_proj_w"].T, dtype=np.float32),
        "opwT": np.ascontiguousarray(inputs["L0_attn_out_proj_w"].T, dtype=np.float32),
        "xpwT": np.ascontiguousarray(inputs["L0_attn_x_proj_w"].T, dtype=np.float32),
        "dtwT": np.ascontiguousarray(inputs["L0_attn_dt_proj_w"].T, dtype=np.float32),
        "convwT": np.ascontiguousarray(
            np.asarray(inputs["L0_attn_conv_w"]).reshape(DI, 4).reshape(DTI, 128, 4)
            .transpose(1, 0, 2).reshape(128, DTI * 4), dtype=np.float32),
        "convbT": np.ascontiguousarray(
            np.asarray(inputs["L0_attn_conv_b"]).reshape(DTI, 128).T, dtype=np.float32),
        "dtbT": np.ascontiguousarray(
            np.asarray(inputs["L0_attn_dt_proj_b"]).reshape(DTI, 128).T, dtype=np.float32),
        "dvecT": np.ascontiguousarray(
            np.asarray(inputs["L0_attn_D"]).reshape(DTI, 128).T, dtype=np.float32),
        "alogT": np.ascontiguousarray(
            np.asarray(inputs["L0_attn_A_log"]).reshape(DTI, 128, DS).transpose(1, 0, 2)
            .reshape(128, DTI * DS), dtype=np.float32),
        "w1": np.ascontiguousarray(inputs["L0_ffn_w1"], dtype=np.float32),
        "w2": np.ascontiguousarray(inputs["L0_ffn_w2"], dtype=np.float32),
        "bias": np.ascontiguousarray(inputs["L0_ffn_bias"], dtype=np.float32).reshape(1, -1),
        "tokT": np.ascontiguousarray(
            (np.arange(2 * TOKS) // 2).astype(np.float32).reshape(8, 128).T),
        "kofT": np.ascontiguousarray(
            (np.arange(2 * TOKS) % 2).astype(np.float32).reshape(8, 128).T),
        "iota256": np.arange(256, dtype=np.float32).reshape(1, -1),
        "eidf": np.arange(8, dtype=np.float32).reshape(8, 1),
        "identb": np.eye(128, dtype=BF),
        "identf": np.eye(128, dtype=np.float32),
    }
    in_maps = []
    for c in range(NCORE):
        t0 = c * TOKS
        ag2rows = np.zeros((128, 4), dtype=np.int32)
        for rt4 in range(4):
            ch = (c * TOKS + rt4 * 128) // TC
            off = (c * TOKS + rt4 * 128) % TC
            ag2rows[:, rt4] = ch * NCORE * TC + off + np.arange(128)
        m = dict(rep)
        m["myx"] = np.ascontiguousarray(x[c])
        m["x0s"] = np.ascontiguousarray(x[0, t0:t0 + TOKS])
        m["idxflat"] = np.ascontiguousarray(ei[t0:t0 + TOKS].reshape(1, -1))
        m["wT"] = np.ascontiguousarray(
            ew[t0:t0 + TOKS].reshape(-1).reshape(8, 128).T.astype(np.float32))
        m["flagcol"] = np.full((128, 1), 1.0 if c == 0 else 0.0, dtype=np.float32)
        m["ag2idx"] = ag2rows
        in_maps.append(m)

    res = run_bass_kernel_spmd(nc, in_maps, core_ids=list(range(NCORE))).results
    out = np.empty((NCORE, L, DM), dtype=np.float32)
    for c in range(NCORE):
        out[c] = res[c]["out_my"]
    for c in range(NCORE):
        out[0, c * TOKS:(c + 1) * TOKS] = res[c]["out_b0"]
    return out


# version 15
# speedup vs baseline: 1.0604x; 1.0604x over previous
import contextlib
import numpy as np
import ml_dtypes
import concourse.bass as bass
import concourse.mybir as mybir
from concourse.bass_utils import run_bass_kernel_spmd
from concourse.bass_core_rust import IndirectOffsetOnAxis

F32 = mybir.dt.float32
BF16 = mybir.dt.bfloat16
I32 = mybir.dt.int32
AF = mybir.ActivationFunctionType
ALU = mybir.AluOpType
AX = mybir.AxisListType
BF = ml_dtypes.bfloat16

NCORE = 8
L, DM, DI, DS, DR, E, FF = 4096, 1024, 2048, 16, 64, 8, 4096
TOKS, CAPL, TC = 512, 192, 256
NCH = L // TC
DTI = DI // 128
FBIG = float(1 << 20)
DEBUG_E = None
DEBUG_MG = None
DEBUG_STOP = None

ENGS = ("v", "s", "t", "g", "y")
PROG = {"V": "v", "S": "s", "T": "t", "G": "g"}


CHAINENG = {"v": "V", "s": "S", "g": "G"}


class Bld:
    def __init__(self):
        self.q = {e: [] for e in ENGS}
        self.cnt = {}
        self.step = {}
        for k in PROG:
            self.cnt[k] = 0
            self.step[k] = 1
        self.chain = set()

    def addsem(self, key, step=16, chained=True):
        self.cnt[key] = 0
        self.step[key] = step
        if chained:
            self.chain.add(key)

    def on(self, eng, fn, waits=(), sig=None, chain=True):
        ev = None
        waits = [w for w in waits if w is not None]
        if sig is not None and sig in self.chain:
            waits.append((sig, self.cnt[sig]))
        pk = CHAINENG.get(eng)
        if pk is not None and (sig is None or sig == pk):
            sig = pk
            if chain:
                waits.append((pk, self.cnt[pk]))
        if sig is not None:
            self.cnt[sig] += self.step[sig]
            ev = (sig, self.cnt[sig])
        self.q[eng].append((tuple(waits), fn, sig))
        return ev

    def now(self, key):
        return (key, self.cnt[key])

    def play(self, name, eng, sems):
        waited = {}
        for waits, fn, sig in self.q[name]:
            mx = {}
            for (k, v) in waits:
                if name == "t" and PROG.get(k) == name:
                    continue
                if v > mx.get(k, 0):
                    mx[k] = v
            for k in mx:
                v = mx[k]
                if waited.get(k, -1) >= v:
                    continue
                eng.wait_ge(sems[k], v)
                waited[k] = v
            inst = fn(eng)
            if sig is not None:
                inst.then_inc(sems[sig], self.step[sig])


def build_program(parts="all"):
    nc = bass.Bass("TRN2", target_bir_lowering=False, detect_race_conditions=False)
    b = Bld()

    def din(name, shape, dt):
        return nc.dram_tensor(name, list(shape), dt, kind="ExternalInput").ap()

    myx = din("myx", (L, DM), F32)
    x0s = din("x0s", (TOKS, DM), F32)
    idxflat = din("idxflat", (1, 2 * TOKS), I32)
    wTd = din("wT", (128, 8), F32)
    flagcol = din("flagcol", (128, 1), F32)
    ag2idx = din("ag2idx", (128, 4), I32)
    ipwT = din("ipwT", (DM, 2 * DI), F32)
    opwT = din("opwT", (DI, DM), F32)
    xpwT = din("xpwT", (DI, 96), F32)
    dtwT = din("dtwT", (DR, DI), F32)
    convwT = din("convwT", (128, DTI * 4), F32)
    convbT = din("convbT", (128, DTI), F32)
    dtbT = din("dtbT", (128, DTI), F32)
    dvecT = din("dvecT", (128, DTI), F32)
    alogT = din("alogT", (128, DTI * DS), F32)
    w1_d = din("w1", (E, DM, FF), F32)
    w2_d = din("w2", (E, FF, DM), F32)
    bias_d = din("bias", (1, DM), F32)
    tokT = din("tokT", (128, 8), F32)
    kofT = din("kofT", (128, 8), F32)
    iota256 = din("iota256", (1, 256), F32)
    eidf = din("eidf", (8, 1), F32)
    identb = din("identb", (128, 128), BF16)
    identf = din("identf", (128, 128), F32)

    out_my = nc.dram_tensor("out_my", [L, DM], F32, kind="ExternalOutput").ap()
    out_b0 = nc.dram_tensor("out_b0", [TOKS, DM], F32, kind="ExternalOutput").ap()

    lnrows_d = nc.dram_tensor("lnrows_d", [TOKS, DM], BF16, kind="Internal").ap()
    outrows_d = nc.dram_tensor("outrows_d", [TOKS + 1, DM], F32, kind="Internal").ap()
    bcB_d = nc.dram_tensor("bcB_d", [16, TC], BF16, kind="Internal").ap()
    bcC_d = nc.dram_tensor("bcC_d", [16, TC], BF16, kind="Internal").ap()
    ag1out = nc.dram_tensor("ag1out", [NCORE * TOKS, DM], F32, kind="Internal",
                            addr_space="Shared").ap()
    ag2in = nc.dram_tensor("ag2in", [L, DM], F32, kind="Internal").ap()
    ag2out = nc.dram_tensor("ag2out", [NCH * NCORE * TC, DM], F32, kind="Internal",
                            addr_space="Shared").ap()
    ipWT_b = nc.dram_tensor("ipWT_b", [DM, 2 * DI], BF16, kind="Internal").ap()
    opWT_b = nc.dram_tensor("opWT_b", [DI, DM], BF16, kind="Internal").ap()

    es = contextlib.ExitStack()
    with es:
        sb = lambda n, sh, dt=F32: es.enter_context(nc.sbuf_tensor(n, list(sh), dt))

        s_identb = sb("c_identb", (128, 128), BF16)
        s_identf = sb("c_identf", (128, 128), F32)
        s_xpw = sb("c_xpw", (128, DTI, 96), BF16)
        s_dtw = sb("c_dtw", (DR, DI), BF16)
        s_convw = sb("c_convw", (128, DTI * 4), F32)
        s_convb = sb("c_convb", (128, DTI), F32)
        s_dtb = sb("c_dtb", (128, DTI), F32)
        s_dvec = sb("c_dvec", (128, DTI), F32)
        s_A2 = sb("c_A2", (128, DTI * DS), F32)
        s_hbias = sb("c_hbias", (128, DM), F32)
        s_flagc = sb("c_flagc", (128, 1), F32)
        s_zeroI = sb("c_zeroI", (128, 1), I32)
        s_hst = sb("c_hst", (128, DTI * DS), F32)
        s_tail = sb("c_tail", (128, DTI * 3), F32)
        s_eid = sb("c_eid", (8, 1), F32)
        s_wT = sb("c_wT", (128, 8), F32)
        s_tok = sb("c_tok", (128, 8), F32)
        s_kof = sb("c_kof", (128, 8), F32)
        s_fbig = sb("c_fbig", (128, 1), F32)
        s_iotaR = sb("c_iotaR", (128, 256), F32)
        s_ag2i = sb("c_ag2i", (128, 4), I32)
        m_rv = sb("c_rv", (128, 16 * 4))
        m_gI = sb("c_gI", (128, 16), I32)
        m_aI = sb("c_aI", (128, 16), I32)
        m_bI = sb("c_bI", (128, 16), I32)

        DMAKEYS = (["CST", "WCA", "HBG", "S1X", "X1L0", "X1L1", "AGL0", "AGL1",
                    "IP0", "IP1", "IP2", "IP3", "OP0", "OP1", "BCD", "BBG", "CBG",
                    "OD0", "OD1", "XD0", "XD1", "FIN", "S3X"] +
                   [t + k for t in ("m1", "m3")
                    for k in ("IDX", "LNR", "PRE", "XG0", "XG1", "SC",
                              "W10", "W11", "W12", "W13",
                              "W20", "W21", "W22", "W23")])
        for k in DMAKEYS:
            b.addsem(k, 16, chained=True)
        b.addsem("CC", 1, chained=True)
        sems = {k: es.enter_context(nc.semaphore("sem_" + k)) for k in b.cnt}
        block = es.enter_context(nc.Block())

        def act(o, i, f, bias=0.0, scale=1.0):
            return lambda e, o=o, i=i, f=f, b_=bias, s_=scale: e.activation(o, i, f, bias=b_, scale=s_)

        def tt(o, i0, i1, op):
            return lambda e, o=o, i0=i0, i1=i1, op=op: e.tensor_tensor(o, i0, i1, op)

        def ts1(o, i0, s1, op0):
            return lambda e, o=o, i0=i0, s1=s1, op0=op0: e.tensor_single_scalar(o, i0, s1, op0)

        def ts2(o, i0, s1, s2, op0, op1):
            return lambda e, o=o, i0=i0, s1=s1, s2=s2, op0=op0, op1=op1: e.tensor_scalar(o, i0, s1, s2, op0, op1)

        def stt(o, i0, s, i1, op0, op1):
            return lambda e, o=o, i0=i0, s=s, i1=i1, op0=op0, op1=op1: e.scalar_tensor_tensor(o, i0, s, i1, op0, op1)

        def cpy(o, i):
            def f(e, o=o, i=i):
                if hasattr(e, "tensor_copy"):
                    return e.tensor_copy(o, i)
                return e.copy(o, i)
            return f

        def sel(o, m, a, z):
            return lambda e, o=o, m=m, a=a, z=z: e.select(o, m, a, z)

        def mm(o, lt, r, st, sp):
            return lambda e, o=o, lt=lt, r=r, st=st, sp=sp: e.matmul(o, lt, r, start=st, stop=sp)

        def trp(o, i, idn):
            return lambda e, o=o, i=i, idn=idn: e.transpose(o, i, idn)

        def dma(o, i):
            return lambda e, o=o, i=i: e.dma_start(o, i)

        def mset(apx, v):
            return lambda e, apx=apx, v=v: e.memset(apx, v)

        def ig(o, i, idxap, bc=None):
            return lambda e, o=o, i=i, idxap=idxap, bc=bc: e.indirect_dma_start(
                o, None, i, IndirectOffsetOnAxis(idxap, 0),
                bounds_check=bc, oob_is_err=False)

        def isca(o, idxap, i, bc):
            return lambda e, o=o, idxap=idxap, i=i, bc=bc: e.indirect_dma_start(
                o, IndirectOffsetOnAxis(idxap, 0), i, None,
                bounds_check=bc, oob_is_err=False, compute_op=ALU.add)

        def rsum(o, i):
            return lambda e, o=o, i=i: e.reduce_sum(o, i, AX.X, None)

        ev_c = []
        for (dst, src) in ((s_identb[:], identb), (s_identf[:], identf),
                           (s_convw[:], convwT), (s_convb[:], convbT),
                           (s_dtb[:], dtbT), (s_dvec[:], dvecT), (s_A2[:], alogT),
                           (s_flagc[:], flagcol), (s_eid[:], eidf), (s_wT[:], wTd),
                           (s_tok[:], tokT), (s_kof[:], kofT), (s_ag2i[:], ag2idx)):
            ev_c.append(b.on("y", dma(dst, src), sig="CST"))
        ev_consts = ev_c[-1]
        ev_ipc = b.on("g", dma(ipWT_b, ipwT), sig="WCA")
        ev_opc = b.on("g", dma(opWT_b, opwT), sig="WCA")
        ev_xpc = b.on("g", lambda e: e.dma_start(
            s_xpw[:, :, :], xpwT.rearrange("(k p) c -> p k c", p=128)), sig="WCA")
        ev_dtc = b.on("g", dma(s_dtw[:], dtwT), sig="WCA")
        ev_a2 = b.on("s", act(s_A2[:], s_A2[:], AF.Exp), waits=[ev_consts], sig="S")
        ev_a2 = b.on("v", ts1(s_A2[:], s_A2[:], -1.0, ALU.mult), waits=[ev_a2], sig="V")
        ev_z = b.on("v", mset(s_zeroI[:], 0), sig="V")
        b.on("v", mset(s_fbig[:], FBIG))
        ev_hb = b.on("g", ig(s_hbias[:], bias_d, s_zeroI[:]), waits=[ev_z], sig="HBG")
        ev_hb = b.on("v", ts1(s_hbias[:], s_hbias[:], 0.5, ALU.mult), waits=[ev_hb], sig="V")
        ev_iotaR = b.on("g", ig(s_iotaR[:], iota256, s_zeroI[:]), waits=[ev_z], sig="HBG")

        def ln_rows(src_ap, out_bf, scratch, col, waits):
            w = list(waits)
            b.on("v", rsum(col[:, 0:1], src_ap), waits=w)
            evq = b.on("s", act(scratch, src_ap, AF.Square), waits=w, sig="S")
            b.on("v", rsum(col[:, 1:2], scratch), waits=[evq])
            b.on("v", ts1(col[:, 0:1], col[:, 0:1], 1.0 / DM, ALU.mult))
            b.on("v", ts1(col[:, 1:2], col[:, 1:2], 1.0 / DM, ALU.mult))
            b.on("v", tt(col[:, 2:3], col[:, 0:1], col[:, 0:1], ALU.mult))
            b.on("v", tt(col[:, 1:2], col[:, 1:2], col[:, 2:3], ALU.subtract))
            ev = b.on("v", ts1(col[:, 1:2], col[:, 1:2], 1e-5, ALU.add), sig="V")
            ev = b.on("s", act(col[:, 3:4], col[:, 1:2], AF.Ln), waits=[ev], sig="S")
            ev = b.on("v", ts1(col[:, 3:4], col[:, 3:4], -0.5, ALU.mult), waits=[ev], sig="V")
            ev = b.on("s", act(col[:, 3:4], col[:, 3:4], AF.Exp), waits=[ev], sig="S")
            ev = b.on("v", ts2(out_bf, src_ap, col[:, 0:1], col[:, 3:4],
                               ALU.subtract, ALU.mult), waits=[ev], sig="V")
            return ev

        st_moe = {}

        def emit_moe(tag, src_tiles, fence, do_routing=True):
            with contextlib.ExitStack() as ar:
                sba = lambda n, sh, dt=F32: ar.enter_context(nc.sbuf_tensor(tag + n, list(sh), dt))
                psa = lambda n, sh, dt=F32: ar.enter_context(nc.psum_tensor(tag + n, list(sh), dt))
                m_idxb = sba("idxb", (8, 2 * TOKS), I32)
                m_idxf = sba("idxf", (8, 2 * TOKS), F32)
                m_mask = sba("mask", (8, 2 * TOKS), F32)
                m_maskI = sba("maskI", (8, 2 * TOKS), I32)
                m_zero8 = sba("zero8", (8, 2 * TOKS), F32)
                m_rank = sba("rank", (8, 2 * TOKS), F32)
                m_nbig8 = sba("nbig8", (8, 2 * TOKS), F32)
                m_rkT = sba("rkT", (128, 64), F32)
                m_vals = sba("vals", (128, 8 * 4), F32)
                m_oh = sba("oh", (128, 2, 128), F32)
                m_gF = sba("gF", (128, 16), F32)
                m_aF = sba("aF", (128, 16), F32)
                m_bF = sba("bF", (128, 16), F32)
                m_zcol = sba("zcol", (128, 1), F32)
                m_dcol = sba("dcol", (128, 1), F32)
                m_mk = sba("mk", (128, 1), F32)
                m_mkI = sba("mkI", (128, 2), I32)
                m_lnsc = sba("lnsc", (128, DM), F32)
                m_lncol = sba("lncol", (128, 8), F32)
                m_lnb = sba("lnb", (128, DM), BF16)
                m_pre = sba("pre", (128, DM), F32)
                m_XG = sba("XG", (128, 2, DM), BF16)
                m_XT = sba("XT", (128, 2, 8 * CAPL), BF16)
                m_hT = sba("hT", (128, 2, 32 * CAPL), BF16)
                m_w1s = sba("w1s", (128, 4, 512), BF16)
                m_w2s = sba("w2s", (128, 4, 512), BF16)
                m_yT = sba("yT", (128, 8 * CAPL), F32)
                m_yr = sba("yr", (128, 2, DM), F32)
                m_yrW = sba("yrW", (128, 2, DM), F32)
                p_rt = psa("prt", (128, 128), F32)
                p_g1t = [psa("pg1%d" % i, (128, 128), BF16) for i in range(2)]
                p_w1t = [psa("pw1%d" % i, (128, 2 * CAPL), F32) for i in range(2)]
                p_w1 = [p_w1t[i // 2][:, (i % 2) * CAPL:(i % 2 + 1) * CAPL] for i in range(4)]
                p_w2t = [psa("pw2%d" % i, (128, 2 * CAPL), F32) for i in range(2)]
                p_w2 = [p_w2t[i // 2][:, (i % 2) * CAPL:(i % 2 + 1) * CAPL] for i in range(4)]
                p_yt = psa("pyt", (128, 512), F32)

                if do_routing:
                    ev_idx = b.on("g", ig(m_idxb[:], idxflat, s_zeroI[0:8, :]),
                                  waits=[ev_z] + fence, sig=tag + "IDX")
                    ev_idx = b.on("v", cpy(m_idxf[:], m_idxb[:]), waits=[ev_idx] + fence, sig="V")
                    b.on("v", mset(m_zero8[:], 0.0), waits=fence)
                    b.on("v", mset(m_nbig8[:], -FBIG))
                    b.on("v", ts1(m_mask[:], m_idxf[:], s_eid[:, 0:1], ALU.is_equal),
                         waits=[ev_consts, ev_idx])
                    b.on("v", cpy(m_maskI[:], m_mask[:]))
                    b.on("v", lambda e: e.tensor_tensor_scan(
                        m_rank[:], m_mask[:], m_zero8[:], 0.0, ALU.add, ALU.add))
                    ev_rs = b.on("v", sel(m_idxf[:], m_maskI[:], m_rank[:], m_nbig8[:]), sig="V")
                    ev_tr_prev = None
                    evs_rk = []
                    for jt in range(8):
                        wt = [ev_rs] + ([ev_tr_prev] if ev_tr_prev else []) + fence
                        evt = b.on("t", trp(p_rt[:, 0:8], m_idxf[:, jt * 128:(jt + 1) * 128],
                                            s_identf[0:8, 0:8]), waits=wt, sig="T")
                        ev_tr_prev = b.on("s", cpy(m_rkT[:, jt * 8:(jt + 1) * 8], p_rt[:, 0:8]),
                                          waits=[evt] + fence, sig="S")
                        evs_rk.append(ev_tr_prev)
                    ev_rkm1 = b.on("v", ts1(m_rkT[:], m_rkT[:], 1.0, ALU.subtract),
                                   waits=evs_rk, sig="V")
                    ev_vals = []
                    for jt in range(8):
                        c0 = jt * 4
                        b.on("v", cpy(m_vals[:, c0:c0 + 1], s_tok[:, jt:jt + 1]),
                             waits=[ev_consts] + fence)
                        b.on("v", ts1(m_vals[:, c0 + 1:c0 + 2], s_wT[:, jt:jt + 1], 0.5, ALU.mult))
                        b.on("v", cpy(m_vals[:, c0 + 2:c0 + 3], s_kof[:, jt:jt + 1]))
                        ev_vals.append(b.on("v", mset(m_vals[:, c0 + 3:c0 + 4], 1.0), sig="V"))
                    ev_rv = []
                    ev_mm_jt = {}
                    for col in range(16):
                        ee, rt = col // 2, col % 2
                        for jt in range(8):
                            woh = [ev_rkm1, ev_iotaR] + ev_vals
                            key = (col, jt - 2) if jt >= 2 else (col - 1, 6 + jt)
                            if key in ev_mm_jt:
                                woh.append(ev_mm_jt[key])
                            evo = b.on("v", ts1(m_oh[:, jt % 2, :],
                                                s_iotaR[:, rt * 128:(rt + 1) * 128],
                                                m_rkT[:, jt * 8 + ee:jt * 8 + ee + 1],
                                                ALU.is_equal), waits=woh, sig="V")
                            wmm = [evo]
                            if jt == 0 and col > 0:
                                wmm.append(ev_rv[col - 1])
                            ev_mm_jt[(col, jt)] = b.on(
                                "t", mm(p_rt[:, 8:12], m_oh[:, jt % 2, :],
                                        m_vals[:, jt * 4:(jt + 1) * 4], jt == 0, jt == 7),
                                waits=wmm, sig="T")
                        ev_rv.append(b.on("s", cpy(m_rv[:, col * 4:(col + 1) * 4], p_rt[:, 8:12]),
                                          waits=[ev_mm_jt[(col, 7)]], sig="S"))
                    ev_route = []
                    b.on("v", mset(m_zcol[:], 0.0), waits=fence)
                    b.on("v", mset(m_dcol[:], float(TOKS)))
                    for col in range(16):
                        ctok = m_rv[:, col * 4:col * 4 + 1]
                        ckp = m_rv[:, col * 4 + 2:col * 4 + 3]
                        cnt = m_rv[:, col * 4 + 3:col * 4 + 4]
                        w = [ev_rv[col]]
                        b.on("v", cpy(m_mkI[:, 0:1], cnt), waits=w)
                        b.on("v", sel(m_gF[:, col:col + 1], m_mkI[:, 0:1], ctok, m_zcol[:, 0:1]))
                        b.on("v", ts1(m_mk[:, 0:1], ckp, 0.5, ALU.is_lt), waits=w)
                        b.on("v", tt(m_mk[:, 0:1], m_mk[:, 0:1], cnt, ALU.mult))
                        b.on("v", cpy(m_mkI[:, 1:2], m_mk[:, 0:1]))
                        b.on("v", sel(m_aF[:, col:col + 1], m_mkI[:, 1:2], ctok, m_dcol[:, 0:1]))
                        b.on("v", ts1(m_mk[:, 0:1], ckp, 0.5, ALU.is_gt), waits=w)
                        b.on("v", tt(m_mk[:, 0:1], m_mk[:, 0:1], cnt, ALU.mult))
                        b.on("v", cpy(m_mkI[:, 1:2], m_mk[:, 0:1]))
                        ev_route.append(b.on("v", sel(m_bF[:, col:col + 1], m_mkI[:, 1:2], ctok,
                                                      m_dcol[:, 0:1]), sig="V"))
                    b.on("v", cpy(m_gI[:], m_gF[:]), waits=ev_route)
                    b.on("v", cpy(m_aI[:], m_aF[:]))
                    ev_gI = b.on("v", cpy(m_bI[:], m_bF[:]), sig="V")

                    st_moe["ev_rv"] = ev_rv[-1]
                    st_moe["ev_gI"] = ev_gI
                    ev_rv_last = ev_rv[-1]
                else:
                    ev_rv_last = st_moe["ev_rv"]
                    ev_gI = st_moe["ev_gI"]
                ev_lnd = []
                ev_pre = []
                ev_lnw = None
                for rt4 in range(4):
                    apx, rdy = src_tiles[rt4]
                    w = [rdy] + fence + ([ev_lnw] if ev_lnw else [])
                    evr = ln_rows(apx, m_lnb[:], m_lnsc[:], m_lncol[:], w)
                    ev_lnw = b.on("y", dma(lnrows_d[rt4 * 128:(rt4 + 1) * 128, :], m_lnb[:]),
                                  waits=[evr], sig=tag + "LNR")
                    ev_lnd.append(ev_lnw)
                    evp = b.on("v", tt(m_pre[:], apx, s_hbias[:], ALU.add),
                               waits=[rdy, ev_hb] + ([ev_pre[-1]] if ev_pre else []), sig="V")
                    ev_pre.append(b.on("y", dma(outrows_d[rt4 * 128:(rt4 + 1) * 128, :],
                                                m_pre[:]), waits=[evp], sig=tag + "PRE"))
                    if rt4 == 3:
                        ev_pre.append(b.on("y", dma(outrows_d[TOKS:TOKS + 1, :],
                                                    m_pre[0:1, :]),
                                           waits=[evp], sig=tag + "PRE"))

                if DEBUG_STOP == "pre":
                    return ev_pre[-1]
                ev_sc_last = None
                ev_w1done = {0: None, 1: None}
                w1ring_ev = [None] * 4
                w2ring_ev = [None] * 4
                ev_yr_used = {0: None, 1: None}
                ev_yt_last = None
                ev_yrp_roll = []
                gath_tail = []
                for ee in range(DEBUG_E or E):
                    pe = ee % 2
                    ev_gath = list(gath_tail)
                    for rt in range(2):
                        col = ee * 2 + rt
                        wm = ([ev_gI] + ([ev_w1done[pe]] if ev_w1done[pe] else []) +
                              ([ev_gath[-1]] if ev_gath else []) + fence)
                        evg = b.on("g", ig(m_XG[:, pe, :], lnrows_d,
                                           m_gI[:, col:col + 1]),
                                   waits=wm + ev_lnd, sig=tag + "XG" + str(pe))
                        for kb in range(8):
                            evt = b.on("t", trp(p_g1t[kb % 2][:],
                                                m_XG[:, pe, kb * 128:(kb + 1) * 128],
                                                s_identb[:]),
                                       waits=[evg] + ([ev_gath[-2]] if len(ev_gath) >= 2
                                                      else []), sig="T")
                            w_ = 128 if rt == 0 else CAPL - 128
                            ev_gath.append(b.on("s", cpy(
                                m_XT[:, pe, kb * CAPL + rt * 128:
                                     kb * CAPL + rt * 128 + w_],
                                p_g1t[kb % 2][:, 0:w_]),
                                waits=[evt], sig="S"))
                    gath_tail = ev_gath[-2:]
                    ev_h = []
                    nw1 = (DEBUG_MG or 8) * 8
                    w1ld = {}

                    def w1load(j, extra=()):
                        ring = j % 4
                        mg_, k_ = j // 8, j % 8
                        w1ld[j] = b.on("g", dma(
                            m_w1s[:, ring, :],
                            w1_d[ee, k_ * 128:(k_ + 1) * 128, mg_ * 512:(mg_ + 1) * 512]),
                            waits=list(extra) +
                                  ([w1ring_ev[ring]] if w1ring_ev[ring] else []),
                            sig=tag + "W1" + str(ring))
                    for j in range(min(4, nw1)):
                        w1load(j)
                    for mg in range(DEBUG_MG or 8):
                        for k in range(8):
                            j = mg * 8 + k
                            ring = j % 4
                            for ml in range(4):
                                ev_mm = b.on("t", mm(
                                    p_w1[ml], m_w1s[:, ring, ml * 128:(ml + 1) * 128],
                                    m_XT[:, pe, k * CAPL:(k + 1) * CAPL],
                                    k == 0 and ml % 2 == 0, k == 7 and ml % 2 == 1),
                                    waits=[w1ld[j]] + (ev_gath if k == 0 else []) +
                                          (ev_h[-4:] if (k == 0 and ev_h) else []),
                                    sig="T" if ml == 3 else None)
                            w1ring_ev[ring] = ev_mm
                            if j + 4 < nw1:
                                w1load(j + 4)
                        for ml in range(4):
                            mt = mg * 4 + ml
                            ev_h.append(b.on("s", act(m_hT[:, pe, mt * CAPL:(mt + 1) * CAPL],
                                                      p_w1[ml], AF.Gelu),
                                             waits=[ev_mm], sig="S"))
                    ev_w1done[pe] = ev_h[-1]
                    if DEBUG_MG:
                        return ev_h[-1]
                    ev_ye = []
                    w2ld = {}

                    def w2load(j):
                        ring = j % 4
                        mg_, k_ = j // 32, j % 32
                        w2ld[j] = b.on("g", dma(
                            m_w2s[:, ring, :],
                            w2_d[ee, k_ * 128:(k_ + 1) * 128, mg_ * 512:(mg_ + 1) * 512]),
                            waits=[w2ring_ev[ring]] if w2ring_ev[ring] else [],
                            sig=tag + "W2" + str(ring))
                    for j in range(4):
                        w2load(j)
                    for mg in range(2):
                        for k in range(32):
                            j = mg * 32 + k
                            ring = j % 4
                            for ml in range(4):
                                ev_mm = b.on("t", mm(
                                    p_w2[ml], m_w2s[:, ring, ml * 128:(ml + 1) * 128],
                                    m_hT[:, pe, k * CAPL:(k + 1) * CAPL],
                                    k == 0 and ml % 2 == 0, k == 31 and ml % 2 == 1),
                                    waits=[w2ld[j]] + (ev_h if k == 0 else []) +
                                          (ev_ye[-4:] if (k == 0 and ev_ye) else []),
                                    sig="T" if ml == 3 else None)
                            w2ring_ev[ring] = ev_mm
                            if j + 4 < 64:
                                w2load(j + 4)
                        for ml in range(4):
                            mt = mg * 4 + ml
                            ev_ye.append(b.on("s", act(m_yT[:, mt * CAPL:(mt + 1) * CAPL],
                                                       p_w2[ml], AF.Identity),
                                              waits=[ev_mm] +
                                                    ([ev_yt_last] if ev_yt_last else []),
                                              sig="S"))
                    for rt in range(2):
                        col = ee * 2 + rt
                        ev_yrp = []
                        w_ = 128 if rt == 0 else CAPL - 128
                        for ml in range(8):
                            pbuf = (p_yt[0:w_, 0:128] if ml % 2 == 0
                                    else p_rt[0:w_, :])
                            evt = b.on("t", trp(pbuf,
                                                m_yT[:, ml * CAPL + rt * 128:
                                                     ml * CAPL + rt * 128 + w_],
                                                s_identf[:]),
                                       waits=ev_ye + [ev_rv_last] +
                                             ([ev_yr_used[rt]] if ev_yr_used[rt] else []) +
                                             ([ev_yrp_roll[-2]] if len(ev_yrp_roll) >= 2
                                              else []),
                                       sig="T")
                            ev_yt_last = evt
                            ev = b.on("s", cpy(
                                m_yr[0:w_, rt, ml * 128:(ml + 1) * 128],
                                pbuf),
                                waits=[evt], sig="S")
                            ev_yrp.append(ev)
                            ev_yrp_roll.append(ev)
                        cw = m_rv[:, col * 4 + 1:col * 4 + 2]
                        evsA = b.on("v", ts1(m_yrW[0:w_, 0, :], m_yr[0:w_, rt, :],
                                             cw[0:w_, :], ALU.mult),
                                    waits=ev_yrp + [ev_gI] +
                                          ([ev_sc_last] if ev_sc_last else []), sig="V")
                        if w_ < 128:
                            evsA = b.on("v", mset(m_yrW[64:128, 0, :], 0.0),
                                        waits=[ev_sc_last] if ev_sc_last else [],
                                        sig="V")
                        wA = [evsA] + ev_pre + ([ev_sc_last] if ev_sc_last else [])
                        evA = b.on("g", isca(outrows_d, m_aI[:, col:col + 1],
                                             m_yrW[:, 0, :], None), waits=wA, sig=tag + "SC")
                        ev_sc_last = b.on("g", isca(outrows_d, m_bI[:, col:col + 1],
                                                    m_yrW[:, 0, :], None),
                                          waits=[evA], sig=tag + "SC")
                        ev_yr_used[rt] = ev_sc_last
                return ev_sc_last

        with contextlib.ExitStack() as s1s:
            s1_x0 = s1s.enter_context(nc.sbuf_tensor("s1_x0", [128, 4, DM], F32))
            s1_tiles = []
            for rt4 in range(4):
                ev = b.on("y", dma(s1_x0[:, rt4, :], x0s[rt4 * 128:(rt4 + 1) * 128, :]),
                          sig="S1X")
                s1_tiles.append((s1_x0[:, rt4, :], ev))
            ev_moe1 = emit_moe("m1", s1_tiles, [])
            if DEBUG_STOP is None and not (DEBUG_E or DEBUG_MG):
                ev_ag1 = b.on("g", lambda e: e.collective_compute(
                    "AllGather", ALU.bypass, replica_groups=[list(range(NCORE))],
                    ins=[outrows_d[0:TOKS, :].opt()], outs=[ag1out.opt()]),
                    waits=[ev_moe1], sig="CC")

        if parts == "s1":
            block.vector(lambda e: b.play("v", e, sems))
            block.scalar(lambda e: b.play("s", e, sems))
            block.tensor(lambda e: b.play("t", e, sems))
            block.gpsimd(lambda e: b.play("g", e, sems))
            block.sync(lambda e: b.play("y", e, sems))
            es.pop_all().close()
            return nc

        fence1 = ([b.now("V"), b.now("S"), b.now("T"), b.now("G"), b.now("S1X")] +
                  [b.now("m1" + k) for k in ("IDX", "LNR", "PRE", "XG0", "XG1", "SC",
                                             "W10", "W11", "W12", "W13",
                                             "W20", "W21", "W22", "W23")])

        with contextlib.ExitStack() as ma:
            sba = lambda n, sh, dt=F32: ma.enter_context(nc.sbuf_tensor(n, list(sh), dt))
            psa = lambda n, sh, dt=F32: ma.enter_context(nc.psum_tensor(n, list(sh), dt))
            a_x1 = sba("a_x1", (128, 2, 2, DM), F32)
            a_lnsc = sba("a_lnsc", (128, DM), F32)
            a_lncol = sba("a_lncol", (128, 8), F32)
            a_lnb = sba("a_lnb", (128, DM), BF16)
            a_lnT = sba("a_lnT", (128, 8, TC), BF16)
            a_ipw = sba("a_ipw", (128, 4, 512), BF16)
            a_xse = sba("a_xse", (128, DTI * (TC + 3)), BF16)
            a_cs = sba("a_cs", (128, 2, TC), F32)
            a_ub = sba("a_ub", (128, DTI * TC), BF16)
            a_uD = sba("a_uD", (128, 2, DTI * TC), BF16)
            a_g = sba("a_g", (128, 2, DTI * TC), BF16)
            a_dtT = sba("a_dtT", (DR, TC), BF16)
            a_bc = sba("a_bc", (32, TC), BF16)
            a_sp = sba("a_sp", (128, TC), F32)
            a_dl = sba("a_dl", (128, 2, DTI * TC), BF16)
            a_du = sba("a_du", (128, 2, DTI * TC), BF16)
            a_Bb = sba("a_Bb", (128, DS * TC), BF16)
            a_Cb = sba("a_Cb", (128, DS * TC), BF16)
            a_dA = sba("a_dA", (128, 4, 4 * TC), BF16)
            a_dBu = sba("a_dBu", (128, 2, 4 * TC), BF16)
            a_h = sba("a_h", (128, 2, 4 * TC), BF16)
            a_hC = sba("a_hC", (128, 2, 8 * TC), BF16)
            a_yp = sba("a_yp", (128, 2, TC), F32)
            a_yp2 = sba("a_yp2", (128, 2, TC), F32)
            a_r1 = sba("a_r1", (128, 2, 4 * TC), BF16)
            a_r2 = sba("a_r2", (128, 2, 2 * TC), BF16)
            a_yf = sba("a_yf", (128, 2, DTI * TC), BF16)
            a_opw = sba("a_opw", (128, 2, 2, 128), BF16)
            a_yo = sba("a_yo", (128, 2, 2 * TC), F32)
            a_x2q = sba("a_x2q", (128, 2, 256), F32)
            a_outq = sba("a_outq", (128, 2, 256), F32)
            p_lnt = [psa("p_ln%d" % i, (128, 128), BF16) for i in range(2)]
            p_ipt = [psa("p_ip%d" % i, (128, 2 * TC), F32) for i in range(2)]
            p_ip = [p_ipt[i // 2][:, (i % 2) * TC:(i % 2 + 1) * TC] for i in range(4)]
            p_xd = psa("p_xd", (128, 2 * TC), F32)[:, 0:TC]
            p_opt = psa("p_op", (128, 2 * TC), F32)
            p_op = [p_opt[:, 0:TC], p_opt[:, TC:2 * TC]]
            p_yot = [psa("p_yo%d" % i, (128, 128), F32) for i in range(2)]

            st = {}
            st["SDv"] = {}; st["SDg"] = {}; st["TDt"] = {}; st["TDv"] = {}
            st["TDs"] = {}; st["DMAq"] = {0: None, 1: None}
            st["lnw"] = None; st["lnT_used"] = None
            st["ipring"] = [None] * 4
            st["opring"] = [None, None]
            st["dA_guard"] = [None] * 4
            st["dB_guard"] = [None, None]
            st["hC_chain"] = [None] * 2
            st["r1_used"] = [None, None]
            st["yp_used"] = [None, None]
            st["yo_used"] = [None, None]
            st["bc_used"] = None
            st["Cb_prev"] = None
            st["blend_prev"] = {0: None, 1: None}
            st["rt_roll"] = []
            st["res_roll"] = []
            st["pdv_prev"] = None
            st["dl_last_prev"] = None
            st["lastip"] = None
            st["lnsc_free"] = None
            st["cs_used"] = [None, None]
            st["xp_done"] = None
            st["x1ev"] = {}
            st["uev"] = {}
            st["gev"] = {}
            st["dlev"] = {}
            st["duev"] = {}
            st["Bbev"] = {}
            st["Cbev"] = {}
            st["yfev"] = {}
            st["x2dma"] = {0: None, 1: None}
            st["outdma"] = {0: None, 1: None}

            def make_head_pieces(c):
                gen = c % 2
                x1ev = {}
                st["x1ev"][c] = x1ev

                def piece_rt(rt):
                    wprev = fence1 if c < 2 else [st["SDv"][c - 2], st["SDg"][c - 2],
                                                  st["TDt"][c - 2], st["TDv"][c - 2],
                                                  st["TDs"][c - 2]]
                    r0 = c * TC + rt * 128
                    evx = b.on("y", dma(a_x1[:, gen, rt, :], myx[r0:r0 + 128, :]),
                               waits=wprev, sig="X1L" + str(rt))
                    eva = b.on("y", dma(a_lnsc[:], ag1out[r0:r0 + 128, :]),
                               waits=[ev_ag1] + ([st["lnsc_free"]] if st["lnsc_free"]
                                                 else []), sig="AGL" + str(rt))
                    ev1 = b.on("v", tt(a_x1[:, gen, rt, :], a_x1[:, gen, rt, :],
                                       s_hbias[:], ALU.add),
                               waits=[evx, ev_hb] + wprev, sig="V")
                    ev1 = b.on("v", tt(a_lnsc[:], a_lnsc[:], a_x1[:, gen, rt, :],
                                       ALU.subtract), waits=[ev1, eva], sig="V")
                    ev1 = b.on("v", stt(a_x1[:, gen, rt, :], a_lnsc[:], s_flagc[:, 0:1],
                                        a_x1[:, gen, rt, :], ALU.mult, ALU.add),
                               waits=[ev1, ev_consts], sig="V")
                    x1ev[rt] = ev1
                    st["blend_prev"][rt] = ev1
                    wln = [ev1] + ([st["lnw"]] if st["lnw"] else [])
                    evl = ln_rows(a_x1[:, gen, rt, :], a_lnb[:], a_lnsc[:], a_lncol[:], wln)
                    st["lnsc_free"] = b.now("V")
                    for kb in range(8):
                        evt = b.on("t", trp(p_lnt[kb % 2][:],
                                            a_lnb[:, kb * 128:(kb + 1) * 128], s_identb[:]),
                                   waits=[evl] + ([st["rt_roll"][-2]]
                                                  if len(st["rt_roll"]) >= 2 else []),
                                   sig="T")
                        st["rt_roll"].append(b.on("s", cpy(
                            a_lnT[:, kb, rt * 128:rt * 128 + 128],
                            p_lnt[kb % 2][:]),
                            waits=[evt] + ([st["lnT_used"]] if st["lnT_used"] else []),
                            sig="S"))
                    st["lnw"] = st["rt_roll"][-1]

                return [lambda rt=rt: piece_rt(rt) for rt in range(2)]

            def make_prod_pieces(c):
                gen = c % 2

                def wpf():
                    return fence1 if c < 2 else [st["SDv"][c - 2], st["SDg"][c - 2],
                                                 st["TDt"][c - 2], st["TDv"][c - 2],
                                                 st["TDs"][c - 2]]
                uev = {}
                gev = {}
                dlev = {}
                duev = {}
                st["uev"][c] = uev
                st["gev"][c] = gev
                st["dlev"][c] = dlev
                st["duev"][c] = duev
                ev_ipev = []
                ipld = {}
                box = {}

                def ipload(eg_, k_):
                    ring = k_ % 4
                    ipld[(eg_, k_)] = b.on("y", dma(
                        a_ipw[:, ring, :],
                        ipWT_b[k_ * 128:(k_ + 1) * 128, eg_ * 512:(eg_ + 1) * 512]),
                        waits=[ev_ipc] + ([st["ipring"][ring]] if st["ipring"][ring]
                                          else []), sig="IP" + str(ring))

                def piece_eg(eg):
                    if eg == 0:
                        for j0 in range(4):
                            ipload(0, j0)
                    ev_rt = st["rt_roll"][-16:]
                    for k in range(8):
                        ring = k % 4
                        for ml in range(4):
                            ev_mm = b.on("t", mm(
                                p_ip[ml], a_ipw[:, ring, ml * 128:(ml + 1) * 128],
                                a_lnT[:, k, :],
                                k == 0 and ml % 2 == 0, k == 7 and ml % 2 == 1),
                                waits=[ipld[(eg, k)]] + (ev_rt if k == 0 else []) +
                                      (ev_ipev[-4:] if (k == 0 and eg > 0) else []) +
                                      ([st["lastip"]] if (k == 0 and eg == 0 and
                                                          st["lastip"]) else []),
                                sig="T" if ml == 3 else None)
                        st["ipring"][ring] = ev_mm
                        nj = eg * 8 + k + 4
                        if nj < 64:
                            ipload(nj // 8, nj % 8)
                    evt = ev_mm
                    for ml in range(4):
                        dt = eg * 4 + ml
                        if dt < DTI:
                            x0 = dt * (TC + 3)
                            u0 = dt * TC
                            cp = dt % 2
                            evc = b.on("s", cpy(a_xse[:, x0 + 3:x0 + 3 + TC], p_ip[ml]),
                                       waits=[evt] + ([st["pdv_prev"]] if st["pdv_prev"]
                                                      else []), sig="S")
                            ev_ipev.append(evc)
                            if c == 0:
                                evtl = b.on("v", mset(a_xse[:, x0:x0 + 3], 0.0),
                                            waits=[st["pdv_prev"]] if st["pdv_prev"]
                                            else [], sig="V")
                            else:
                                evtl = b.on("v", cpy(a_xse[:, x0:x0 + 3],
                                                     s_tail[:, dt * 3:dt * 3 + 3]),
                                            sig="V")
                            b.on("v", cpy(s_tail[:, dt * 3:dt * 3 + 3],
                                          a_xse[:, x0 + TC:x0 + TC + 3]), waits=[evc])
                            b.on("v", ts1(a_cs[:, cp, :], a_xse[:, x0 + 3:x0 + 3 + TC],
                                          s_convw[:, dt * 4 + 3:dt * 4 + 4], ALU.mult),
                                 waits=[evc, evtl] + ([st["cs_used"][cp]]
                                                      if st["cs_used"][cp] else []))
                            for j in (2, 1):
                                b.on("v", stt(a_cs[:, cp, :], a_xse[:, x0 + j:x0 + j + TC],
                                              s_convw[:, dt * 4 + j:dt * 4 + j + 1],
                                              a_cs[:, cp, :], ALU.mult, ALU.add))
                            evcv = b.on("v", stt(a_cs[:, cp, :], a_xse[:, x0:x0 + TC],
                                                 s_convw[:, dt * 4:dt * 4 + 1],
                                                 a_cs[:, cp, :], ALU.mult, ALU.add),
                                        sig="V")
                            evsl = b.on("s", act(a_ub[:, u0:u0 + TC], a_cs[:, cp, :],
                                                 AF.Silu, bias=s_convb[:, dt:dt + 1]),
                                        waits=[evcv, ev_consts] +
                                              ([st["xp_done"]] if st["xp_done"] else []),
                                        sig="S")
                            st["cs_used"][cp] = evsl
                            evud = b.on("v", ts1(a_uD[:, gen, u0:u0 + TC],
                                                 a_ub[:, u0:u0 + TC],
                                                 s_dvec[:, dt:dt + 1], ALU.mult),
                                        waits=[evsl] + wpf(), sig="V", chain=False)
                            uev[dt] = (evsl, evud)
                        else:
                            dr = dt - DTI
                            u0 = dr * TC
                            gev[dr] = b.on("s", act(a_g[:, gen, u0:u0 + TC], p_ip[ml],
                                                    AF.Silu), waits=[evt] + wpf(), sig="S")
                            ev_ipev.append(gev[dr])
                    if eg == 7:
                        st["lastip"] = b.now("S")
                        st["lnT_used"] = b.now("T")

                def piece_xp():
                    for k in range(DTI):
                        ev_mm = b.on("t", mm(p_xd[0:96], s_xpw[:, k, :],
                                             a_ub[:, k * TC:(k + 1) * TC],
                                             k == 0, k == DTI - 1),
                                     waits=[ev_xpc, uev[k][0]] +
                                           ([st["bc_used"]] if (k == 0 and st["bc_used"])
                                            else []) +
                                           ([st["dl_last_prev"]] if (k == 0 and
                                                                     st["dl_last_prev"])
                                            else []),
                                     sig="T" if k == DTI - 1 else None)
                    evt = ev_mm
                    st["xp_done"] = evt
                    ev_dtT = b.on("s", cpy(a_dtT[:, :], p_xd[0:64]), waits=[evt], sig="S")
                    ev_bcv = b.on("s", cpy(a_bc[:, :], p_xd[64:96]), waits=[evt], sig="S")
                    wbc = [ev_bcv] + ([st["Cb_prev"]] if st["Cb_prev"] else [])
                    b.on("y", dma(bcB_d, a_bc[0:16, :]), waits=wbc, sig="BCD")
                    ev_bcd = b.on("y", dma(bcC_d, a_bc[16:32, :]), sig="BCD")
                    st["bc_used"] = ev_bcd
                    box["bcd"] = ev_bcd
                    box["dtT"] = ev_dtT
                    box["bcv"] = ev_bcv

                def piece_dt(q):
                    prev_dl = box.get("prev_dl")
                    for mt in range(q * 4, q * 4 + 4):
                        evmm = b.on("t", mm(p_xd, s_dtw[:, mt * 128:(mt + 1) * 128],
                                            a_dtT[:, :], True, True),
                                    waits=[ev_dtc, box["dtT"], box["bcv"]] +
                                          ([prev_dl] if prev_dl else []), sig="T")
                        d0 = mt * TC
                        evsp = b.on("s", act(a_sp[:, :], p_xd, AF.Exp,
                                             bias=s_dtb[:, mt:mt + 1]),
                                    waits=[evmm, ev_consts], sig="S")
                        dlev[mt] = b.on("s", act(a_dl[:, gen, d0:d0 + TC], a_sp[:, :],
                                                 AF.Ln, bias=1.0),
                                        waits=[evsp] + wpf(), sig="S")
                        prev_dl = dlev[mt]
                        duev[mt] = b.on("v", tt(a_du[:, gen, d0:d0 + TC],
                                                a_dl[:, gen, d0:d0 + TC],
                                                a_ub[:, d0:d0 + TC], ALU.mult),
                                        waits=[dlev[mt]] + wpf(), sig="V",
                                        chain=False)
                    box["prev_dl"] = prev_dl
                    if q == 3:
                        st["dl_last_prev"] = dlev[DTI - 1]
                        st["pdv_prev"] = b.now("V")

                def piece_gather():
                    st["Bbev"][c] = b.on("g", ig(a_Bb[:, :], bcB_d, s_zeroI[:]),
                                         waits=[box["bcd"], ev_z], sig="BBG")
                    st["Cbev"][c] = b.on("g", ig(a_Cb[:, :], bcC_d, s_zeroI[:]),
                                         waits=[box["bcd"], b.now("V")], sig="CBG")
                    st["Cb_prev"] = st["Cbev"][c]

                return ([lambda eg=eg: piece_eg(eg) for eg in range(8)] +
                        [piece_xp] +
                        [lambda q=q: piece_dt(q) for q in range(4)] +
                        [None, None, piece_gather])

            def emit_scan(c, pieces):
                gen = c % 2
                dlev = st["dlev"][c]
                duev = st["duev"][c]
                uev = st["uev"][c]
                gev = st["gev"][c]
                wtd = [st["TDt"][c - 2]] if c >= 2 else []
                yfev = []

                def stage_a(dt):
                    d0 = dt * TC
                    halves = []
                    for half in range(2):
                        hp = (dt * 2 + half) % 2
                        whc = [st["hC_chain"][hp]] if st["hC_chain"][hp] else []
                        hc_evs = []
                        for gl in range(2):
                            grp = half * 2 + gl
                            pg = grp % 2
                            n0 = grp * 4
                            ev_dA = None
                            for nl in range(4):
                                n = n0 + nl
                                ev_dA = b.on("s", act(
                                    a_dA[:, grp, nl * TC:(nl + 1) * TC],
                                    a_dl[:, gen, d0:d0 + TC], AF.Exp,
                                    scale=s_A2[:, dt * DS + n:dt * DS + n + 1]),
                                    waits=[dlev[dt], ev_a2] +
                                          ([st["dA_guard"][grp]]
                                           if st["dA_guard"][grp] else []),
                                    sig="S" if nl == 3 else None)
                            ev_dB = b.on("g", lambda e, pg=pg, gen=gen, d0=d0, n0=n0:
                                         e.tensor_tensor(
                                             a_dBu[:, pg, :].rearrange(
                                                 "p (n t) -> p n t", n=4),
                                             a_du[:, gen, d0:d0 + TC].rearrange(
                                                 "p (o t) -> p o t", o=1).broadcast_to(
                                                 (128, 4, TC)),
                                             a_Bb[:, n0 * TC:(n0 + 4) * TC].rearrange(
                                                 "p (n t) -> p n t", n=4), ALU.mult),
                                         waits=[duev[dt], st["Bbev"][c]] +
                                               ([st["dB_guard"][pg]]
                                                if st["dB_guard"][pg] else []),
                                         sig="G", chain=False)
                            ev_sc = None
                            for nl in range(4):
                                n = n0 + nl
                                ini = (0.0 if c == 0 else
                                       s_hst[:, dt * DS + n:dt * DS + n + 1])
                                ev_sc = b.on("v", lambda e, grp=grp, pg=pg, nl=nl,
                                             ini=ini:
                                             e.tensor_tensor_scan(
                                                 a_h[:, pg, nl * TC:(nl + 1) * TC],
                                                 a_dA[:, grp, nl * TC:(nl + 1) * TC],
                                                 a_dBu[:, pg, nl * TC:(nl + 1) * TC],
                                                 ini, ALU.mult, ALU.add),
                                             waits=[ev_dA, ev_dB], sig="V",
                                             chain=False)
                            st["dA_guard"][grp] = ev_sc
                            st["dB_guard"][pg] = ev_sc
                            b.on("v", lambda e, pg=pg, dt=dt, n0=n0: e.tensor_copy(
                                s_hst[:, dt * DS + n0:dt * DS + n0 + 4],
                                a_h[:, pg, :].rearrange("p (n t) -> p n t", t=TC)
                                [:, :, TC - 1:TC].rearrange("p n one -> p (n one)")))
                            evhc = b.on("v", lambda e, pg=pg, hp=hp, gl=gl, n0=n0:
                                        e.tensor_tensor(
                                            a_hC[:, hp, gl * 4 * TC:(gl + 1) * 4 * TC],
                                            a_h[:, pg, :],
                                            a_Cb[:, n0 * TC:(n0 + 4) * TC], ALU.mult),
                                        waits=[st["Cbev"][c], ev_sc] + whc, sig="V",
                                        chain=False)
                            hc_evs.append(evhc)
                        halves.append((hp, hc_evs))
                    return halves

                def stage_l1(dt, halves):
                    l1evs = []
                    for half in range(2):
                        hp, hc_evs = halves[half]
                        hp2 = (dt * 2 + half) % 2
                        ev_l1 = b.on("v", tt(a_r1[:, hp2, :],
                                             a_hC[:, hp, 0:4 * TC],
                                             a_hC[:, hp, 4 * TC:8 * TC], ALU.add),
                                     waits=hc_evs + ([st["r1_used"][hp2]]
                                                     if st["r1_used"][hp2] else []),
                                     sig="V", chain=False)
                        st["hC_chain"][hp] = ev_l1
                        l1evs.append(ev_l1)
                    return l1evs

                def stage_tree(dt, l1evs):
                    d0 = dt * TC
                    dp = dt % 2
                    ev_red = []
                    for half in range(2):
                        hp2 = (dt * 2 + half) % 2
                        ypbuf = a_yp if half == 0 else a_yp2
                        ev_l2 = b.on("v", tt(a_r2[:, hp2, :],
                                             a_r1[:, hp2, 0:2 * TC],
                                             a_r1[:, hp2, 2 * TC:4 * TC], ALU.add),
                                     waits=[l1evs[half]], sig="V", chain=False)
                        st["r1_used"][hp2] = ev_l2
                        evrd = b.on("v", tt(ypbuf[:, dp, :],
                                            a_r2[:, hp2, 0:TC],
                                            a_r2[:, hp2, TC:2 * TC], ALU.add),
                                    waits=[st["yp_used"][dp]]
                                          if (half == 0 and st["yp_used"][dp])
                                          else [], sig="V")
                        ev_red.append(evrd)
                    evy = b.on("v", tt(a_yp[:, dp, :], a_yp[:, dp, :],
                                       a_yp2[:, dp, :], ALU.add),
                               waits=ev_red, sig="V", chain=False)
                    evy = b.on("v", tt(a_yp[:, dp, :], a_yp[:, dp, :],
                                       a_uD[:, gen, d0:d0 + TC], ALU.add),
                               waits=[uev[dt][1]], sig="V")
                    evy = b.on("v", tt(a_yf[:, gen, d0:d0 + TC], a_yp[:, dp, :],
                                       a_g[:, gen, d0:d0 + TC], ALU.mult),
                               waits=[gev[dt]] + wtd, sig="V")
                    st["yp_used"][dp] = evy
                    yfev.append(evy)

                prev = None
                for dt in range(DTI):
                    halves = stage_a(dt)
                    if prev is not None:
                        pdt, phalves = prev
                        l1evs = stage_l1(pdt, phalves)
                        stage_tree(pdt, l1evs)
                    prev = (dt, halves)
                    for p in pieces[dt]:
                        p()
                pdt, phalves = prev
                l1evs = stage_l1(pdt, phalves)
                stage_tree(pdt, l1evs)
                st["SDv"][c] = yfev[-1]
                st["SDg"][c] = b.now("G")
                st["yfev"][c] = yfev

            def make_tail_pieces(c):
                gen = c % 2
                yfev = st["yfev"][c]
                x1ev = st["x1ev"][c]
                evs_out = []
                opld = {}

                def opload(hf_, k_):
                    ring = k_ % 2
                    opld[(hf_, k_)] = b.on("y", dma(
                        a_opw[:, ring, :, :].rearrange("p a b -> p (a b)"),
                        opWT_b[k_ * 128:(k_ + 1) * 128, hf_ * 256:(hf_ + 1) * 256]),
                        waits=[ev_opc] + ([st["opring"][ring]] if st["opring"][ring]
                                          else []), sig="OP" + str(ring))

                def piece_hf(hf):
                    if hf == 0:
                        opload(0, 0)
                        opload(0, 1)
                    prev_oev = st.get("oev_prev", [])
                    yp_ = hf % 2
                    ev_oev = []
                    for k in range(DTI):
                        ring = k % 2
                        for ml in range(2):
                            ev_mm = b.on("t", mm(
                                p_op[ml], a_opw[:, ring, ml, :],
                                a_yf[:, gen, k * TC:(k + 1) * TC],
                                k == 0 and ml == 0, k == DTI - 1 and ml == 1),
                                waits=[opld[(hf, k)]] + (yfev if k == 0 else []) +
                                      (prev_oev if k == 0 else []),
                                sig="T" if ml == 1 else None)
                        st["opring"][ring] = ev_mm
                        nj = hf * DTI + k + 2
                        if nj < 4 * DTI:
                            opload(nj // DTI, nj % DTI)
                    evt = ev_mm
                    for ml in range(2):
                        ev_oev.append(b.on("s", act(a_yo[:, yp_, ml * TC:(ml + 1) * TC],
                                                    p_op[ml], AF.Identity),
                                           waits=[evt] + ([st["yo_used"][yp_]]
                                                          if st["yo_used"][yp_] else []),
                                           sig="S"))
                    for rt in range(2):
                        r0 = c * TC + rt * 128
                        qp = (hf * 2 + rt) % 2
                        ev_tr = []
                        for dq in range(2):
                            evtr = b.on("t", trp(p_yot[dq][:],
                                                 a_yo[:, yp_, dq * TC + rt * 128:
                                                      dq * TC + rt * 128 + 128],
                                                 s_identf[:]),
                                        waits=ev_oev + ([st["res_roll"][-2]]
                                                        if len(st["res_roll"]) >= 2
                                                        else []), sig="T")
                            evr = b.on("v", tt(a_x2q[:, qp, dq * 128:(dq + 1) * 128],
                                               p_yot[dq][:],
                                               a_x1[:, gen, rt,
                                                    hf * 256 + dq * 128:
                                                    hf * 256 + (dq + 1) * 128], ALU.add),
                                       waits=[evtr, x1ev[rt]] +
                                             ([st["x2dma"][qp]] if st["x2dma"][qp]
                                              else []), sig="V", chain=False)
                            ev_tr.append(evr)
                            st["res_roll"].append(evr)
                        evo = b.on("v", tt(a_outq[:, qp, :], a_x2q[:, qp, :],
                                           s_hbias[:, hf * 256:(hf + 1) * 256], ALU.add),
                               waits=ev_tr + ([st["outdma"][qp]] if st["outdma"][qp]
                                              else []), sig="V")
                        evd1 = b.on("y", dma(out_my[r0:r0 + 128,
                                                    hf * 256:(hf + 1) * 256],
                                             a_outq[:, qp, :]), waits=[evo], sig="OD" + str(qp))
                        st["outdma"][qp] = evd1
                        evd2 = b.on("y", dma(ag2in[r0:r0 + 128, hf * 256:(hf + 1) * 256],
                                             a_x2q[:, qp, :]), waits=[ev_tr[-1]],
                                    sig="XD" + str(qp))
                        st["x2dma"][qp] = evd2
                        evs_out.extend([evd1, evd2])
                    st["yo_used"][yp_] = b.now("T")
                    st["oev_prev"] = ev_oev
                    if hf == 3:
                        st["TDt"][c] = b.now("T")
                        st["TDv"][c] = b.now("V")
                        st["TDs"][c] = b.now("S")
                        b.on("g", lambda e, c=c: e.collective_compute(
                            "AllGather", ALU.bypass,
                            replica_groups=[list(range(NCORE))],
                            ins=[ag2in[c * TC:(c + 1) * TC, :].opt()],
                            outs=[ag2out[c * NCORE * TC:
                                         (c + 1) * NCORE * TC, :].opt()]),
                            waits=evs_out, sig="CC")
                        st["lastout"] = evs_out[-1]

                return [lambda hf=hf: piece_hf(hf) for hf in range(4)]

            PRODW = ((6, 0), (7, 1), (8, 2), (9, 3), (10, 8), (11, 4), (11, 9),
                     (12, 5), (12, 10), (13, 6), (13, 11), (14, 7), (14, 12),
                     (15, 15))
            for p in make_head_pieces(0):
                p()
            pp0 = make_prod_pieces(0)
            for _, j in PRODW:
                if pp0[j] is not None:
                    pp0[j]()
            for c in range(NCH):
                slots = [[] for _ in range(DTI)]
                if c >= 1:
                    tps = make_tail_pieces(c - 1)
                    for i in range(4):
                        slots[i].append(tps[i])
                if c + 1 < NCH:
                    hps = make_head_pieces(c + 1)
                    slots[4].append(hps[0])
                    slots[5].append(hps[1])
                    pps = make_prod_pieces(c + 1)
                    for w, j in PRODW:
                        if pps[j] is not None:
                            slots[w].append(pps[j])
                emit_scan(c, slots)
            for p in make_tail_pieces(NCH - 1):
                p()

            fence2 = ([b.now("V"), b.now("S"), b.now("T"), b.now("G"), b.now("CC"),
                       st["lastout"]] +
                      [b.now(k) for k in ("X1L0", "X1L1", "AGL0", "AGL1", "IP0", "IP1",
                                          "OP0", "OP1", "BCD", "BBG", "CBG",
                                          "OD0", "OD1", "XD0", "XD1")])

        with contextlib.ExitStack() as s3s:
            s3_x2 = s3s.enter_context(nc.sbuf_tensor("s3_x2", [128, 4, DM], F32))
            s3_tiles = []
            for rt4 in range(4):
                ev = b.on("g", ig(s3_x2[:, rt4, :], ag2out, s_ag2i[:, rt4:rt4 + 1]),
                          waits=fence2 + [ev_consts], sig="S3X")
                s3_tiles.append((s3_x2[:, rt4, :], ev))
            ev_moe3 = emit_moe("m3", s3_tiles, fence2, do_routing=False)
            ev_fin = b.on("y", dma(out_b0, outrows_d[0:TOKS, :]), waits=[ev_moe3], sig="FIN")
            b.on("y", lambda e: e.nop(), waits=[ev_fin, b.now("OD0"), b.now("OD1"), b.now("m3SC")])

        block.vector(lambda e: b.play("v", e, sems))
        block.scalar(lambda e: b.play("s", e, sems))
        block.tensor(lambda e: b.play("t", e, sems))
        block.gpsimd(lambda e: b.play("g", e, sems))
        block.sync(lambda e: b.play("y", e, sems))

    return nc


_CACHE = {}


def kernel(**inputs):
    x = np.ascontiguousarray(inputs["L0_attn_x"], dtype=np.float32)
    ei = np.ascontiguousarray(inputs["L0_ffn_expert_indices"], dtype=np.int32)
    ew = np.ascontiguousarray(inputs["L0_ffn_expert_weights"], dtype=np.float32)
    _ = inputs.get("L0_ffn_scores")

    if "nc" not in _CACHE:
        _CACHE["nc"] = build_program()
    nc = _CACHE["nc"]

    rep = {
        "ipwT": np.ascontiguousarray(inputs["L0_attn_in_proj_w"].T, dtype=np.float32),
        "opwT": np.ascontiguousarray(inputs["L0_attn_out_proj_w"].T, dtype=np.float32),
        "xpwT": np.ascontiguousarray(inputs["L0_attn_x_proj_w"].T, dtype=np.float32),
        "dtwT": np.ascontiguousarray(inputs["L0_attn_dt_proj_w"].T, dtype=np.float32),
        "convwT": np.ascontiguousarray(
            np.asarray(inputs["L0_attn_conv_w"]).reshape(DI, 4).reshape(DTI, 128, 4)
            .transpose(1, 0, 2).reshape(128, DTI * 4), dtype=np.float32),
        "convbT": np.ascontiguousarray(
            np.asarray(inputs["L0_attn_conv_b"]).reshape(DTI, 128).T, dtype=np.float32),
        "dtbT": np.ascontiguousarray(
            np.asarray(inputs["L0_attn_dt_proj_b"]).reshape(DTI, 128).T, dtype=np.float32),
        "dvecT": np.ascontiguousarray(
            np.asarray(inputs["L0_attn_D"]).reshape(DTI, 128).T, dtype=np.float32),
        "alogT": np.ascontiguousarray(
            np.asarray(inputs["L0_attn_A_log"]).reshape(DTI, 128, DS).transpose(1, 0, 2)
            .reshape(128, DTI * DS), dtype=np.float32),
        "w1": np.ascontiguousarray(inputs["L0_ffn_w1"], dtype=np.float32),
        "w2": np.ascontiguousarray(inputs["L0_ffn_w2"], dtype=np.float32),
        "bias": np.ascontiguousarray(inputs["L0_ffn_bias"], dtype=np.float32).reshape(1, -1),
        "tokT": np.ascontiguousarray(
            (np.arange(2 * TOKS) // 2).astype(np.float32).reshape(8, 128).T),
        "kofT": np.ascontiguousarray(
            (np.arange(2 * TOKS) % 2).astype(np.float32).reshape(8, 128).T),
        "iota256": np.arange(256, dtype=np.float32).reshape(1, -1),
        "eidf": np.arange(8, dtype=np.float32).reshape(8, 1),
        "identb": np.eye(128, dtype=BF),
        "identf": np.eye(128, dtype=np.float32),
    }
    in_maps = []
    for c in range(NCORE):
        t0 = c * TOKS
        ag2rows = np.zeros((128, 4), dtype=np.int32)
        for rt4 in range(4):
            ch = (c * TOKS + rt4 * 128) // TC
            off = (c * TOKS + rt4 * 128) % TC
            ag2rows[:, rt4] = ch * NCORE * TC + off + np.arange(128)
        m = dict(rep)
        m["myx"] = np.ascontiguousarray(x[c])
        m["x0s"] = np.ascontiguousarray(x[0, t0:t0 + TOKS])
        m["idxflat"] = np.ascontiguousarray(ei[t0:t0 + TOKS].reshape(1, -1))
        m["wT"] = np.ascontiguousarray(
            ew[t0:t0 + TOKS].reshape(-1).reshape(8, 128).T.astype(np.float32))
        m["flagcol"] = np.full((128, 1), 1.0 if c == 0 else 0.0, dtype=np.float32)
        m["ag2idx"] = ag2rows
        in_maps.append(m)

    res = run_bass_kernel_spmd(nc, in_maps, core_ids=list(range(NCORE))).results
    out = np.empty((NCORE, L, DM), dtype=np.float32)
    for c in range(NCORE):
        out[c] = res[c]["out_my"]
    for c in range(NCORE):
        out[0, c * TOKS:(c + 1) * TOKS] = res[c]["out_b0"]
    return out


# version 17 (final)
# speedup vs baseline: 1.0919x; 1.0297x over previous
import contextlib
import numpy as np
import ml_dtypes
import concourse.bass as bass
import concourse.mybir as mybir
from concourse.bass_utils import run_bass_kernel_spmd
from concourse.bass_core_rust import IndirectOffsetOnAxis

F32 = mybir.dt.float32
BF16 = mybir.dt.bfloat16
I32 = mybir.dt.int32
AF = mybir.ActivationFunctionType
ALU = mybir.AluOpType
AX = mybir.AxisListType
BF = ml_dtypes.bfloat16

NCORE = 8
L, DM, DI, DS, DR, E, FF = 4096, 1024, 2048, 16, 64, 8, 4096
TOKS, CAPL, TC = 512, 192, 256
NCH = L // TC
DTI = DI // 128
FBIG = float(1 << 20)
DEBUG_E = None
DEBUG_MG = None
DEBUG_STOP = None

ENGS = ("v", "s", "t", "g", "y")
PROG = {"V": "v", "S": "s", "T": "t", "G": "g"}


CHAINENG = {"v": "V", "s": "S", "g": "G"}


class Bld:
    def __init__(self):
        self.q = {e: [] for e in ENGS}
        self.cnt = {}
        self.step = {}
        for k in PROG:
            self.cnt[k] = 0
            self.step[k] = 1
        self.chain = set()

    def addsem(self, key, step=16, chained=True):
        self.cnt[key] = 0
        self.step[key] = step
        if chained:
            self.chain.add(key)

    def on(self, eng, fn, waits=(), sig=None, chain=True):
        ev = None
        waits = [w for w in waits if w is not None]
        if sig is not None and sig in self.chain:
            waits.append((sig, self.cnt[sig]))
        pk = CHAINENG.get(eng)
        if pk is not None and (sig is None or sig == pk):
            sig = pk
            if chain and eng != "s":
                waits.append((pk, self.cnt[pk]))
        if sig is not None:
            self.cnt[sig] += self.step[sig]
            ev = (sig, self.cnt[sig])
        self.q[eng].append((tuple(waits), fn, sig))
        return ev

    def now(self, key):
        return (key, self.cnt[key])

    def play(self, name, eng, sems):
        waited = {}
        for waits, fn, sig in self.q[name]:
            mx = {}
            for (k, v) in waits:
                if name == "t" and PROG.get(k) == name:
                    continue
                if v > mx.get(k, 0):
                    mx[k] = v
            for k in mx:
                v = mx[k]
                if waited.get(k, -1) >= v:
                    continue
                eng.wait_ge(sems[k], v)
                waited[k] = v
            inst = fn(eng)
            if sig is not None:
                inst.then_inc(sems[sig], self.step[sig])


def build_program(parts="all"):
    nc = bass.Bass("TRN2", target_bir_lowering=False, detect_race_conditions=False)
    b = Bld()

    def din(name, shape, dt):
        return nc.dram_tensor(name, list(shape), dt, kind="ExternalInput").ap()

    myx = din("myx", (L, DM), F32)
    x0s = din("x0s", (TOKS, DM), F32)
    idxflat = din("idxflat", (1, 2 * TOKS), I32)
    wTd = din("wT", (128, 8), F32)
    flagcol = din("flagcol", (128, 1), F32)
    ag2idx = din("ag2idx", (128, 4), I32)
    ipwT = din("ipwT", (DM, 2 * DI), F32)
    opwT = din("opwT", (DI, DM), F32)
    xpwT = din("xpwT", (DI, 96), F32)
    dtwT = din("dtwT", (DR, DI), F32)
    convwT = din("convwT", (128, DTI * 4), F32)
    convbT = din("convbT", (128, DTI), F32)
    dtbT = din("dtbT", (128, DTI), F32)
    dvecT = din("dvecT", (128, DTI), F32)
    alogT = din("alogT", (128, DTI * DS), F32)
    w1_d = din("w1", (E, DM, FF), F32)
    w2_d = din("w2", (E, FF, DM), F32)
    bias_d = din("bias", (1, DM), F32)
    tokT = din("tokT", (128, 8), F32)
    kofT = din("kofT", (128, 8), F32)
    iota256 = din("iota256", (1, 256), F32)
    eidf = din("eidf", (8, 1), F32)
    identb = din("identb", (128, 128), BF16)
    identf = din("identf", (128, 128), F32)

    out_my = nc.dram_tensor("out_my", [L, DM], F32, kind="ExternalOutput").ap()
    out_b0 = nc.dram_tensor("out_b0", [TOKS, DM], F32, kind="ExternalOutput").ap()

    lnrows_d = nc.dram_tensor("lnrows_d", [TOKS, DM], BF16, kind="Internal").ap()
    outrows_d = nc.dram_tensor("outrows_d", [TOKS + 1, DM], F32, kind="Internal").ap()
    bcB_d = nc.dram_tensor("bcB_d", [16, TC], BF16, kind="Internal").ap()
    bcC_d = nc.dram_tensor("bcC_d", [16, TC], BF16, kind="Internal").ap()
    ag1out = nc.dram_tensor("ag1out", [NCORE * TOKS, DM], F32, kind="Internal",
                            addr_space="Shared").ap()
    ag2in = nc.dram_tensor("ag2in", [L, DM], F32, kind="Internal").ap()
    ag2out = nc.dram_tensor("ag2out", [NCH * NCORE * TC, DM], F32, kind="Internal",
                            addr_space="Shared").ap()
    ipWT_b = nc.dram_tensor("ipWT_b", [DM, 2 * DI], BF16, kind="Internal").ap()
    opWT_b = nc.dram_tensor("opWT_b", [DI, DM], BF16, kind="Internal").ap()

    es = contextlib.ExitStack()
    with es:
        sb = lambda n, sh, dt=F32: es.enter_context(nc.sbuf_tensor(n, list(sh), dt))

        s_identb = sb("c_identb", (128, 128), BF16)
        s_identf = sb("c_identf", (128, 128), F32)
        s_xpw = sb("c_xpw", (128, DTI, 96), BF16)
        s_dtw = sb("c_dtw", (DR, DI), BF16)
        s_convw = sb("c_convw", (128, DTI * 4), F32)
        s_convb = sb("c_convb", (128, DTI), F32)
        s_dtb = sb("c_dtb", (128, DTI), F32)
        s_dvec = sb("c_dvec", (128, DTI), F32)
        s_A2 = sb("c_A2", (128, DTI * DS), F32)
        s_hbias = sb("c_hbias", (128, DM), F32)
        s_flagc = sb("c_flagc", (128, 1), F32)
        s_zeroI = sb("c_zeroI", (128, 1), I32)
        s_hst = sb("c_hst", (128, DTI * DS), F32)
        s_tail = sb("c_tail", (128, DTI * 3), F32)
        s_eid = sb("c_eid", (8, 1), F32)
        s_wT = sb("c_wT", (128, 8), F32)
        s_tok = sb("c_tok", (128, 8), F32)
        s_kof = sb("c_kof", (128, 8), F32)
        s_fbig = sb("c_fbig", (128, 1), F32)
        s_iotaR = sb("c_iotaR", (128, 256), F32)
        s_ag2i = sb("c_ag2i", (128, 4), I32)
        m_rv = sb("c_rv", (128, 16 * 4))
        m_gI = sb("c_gI", (128, 16), I32)
        m_aI = sb("c_aI", (128, 16), I32)
        m_bI = sb("c_bI", (128, 16), I32)

        DMAKEYS = (["CST", "WCA", "HBG", "S1X", "X1L0", "X1L1", "AGL0", "AGL1",
                    "IP0", "IP1", "IP2", "IP3", "OP0", "OP1", "BCD", "BBG", "CBG",
                    "OD0", "OD1", "XD0", "XD1", "FIN", "S3X"] +
                   [t + k for t in ("m1", "m3")
                    for k in ("IDX", "LNR", "PRE", "XG0", "XG1", "SC",
                              "W10", "W11", "W12", "W13",
                              "W20", "W21", "W22", "W23")])
        for k in DMAKEYS:
            b.addsem(k, 16, chained=True)
        b.addsem("CC", 1, chained=True)
        sems = {k: es.enter_context(nc.semaphore("sem_" + k)) for k in b.cnt}
        block = es.enter_context(nc.Block())

        def act(o, i, f, bias=0.0, scale=1.0):
            return lambda e, o=o, i=i, f=f, b_=bias, s_=scale: e.activation(o, i, f, bias=b_, scale=s_)

        def tt(o, i0, i1, op):
            return lambda e, o=o, i0=i0, i1=i1, op=op: e.tensor_tensor(o, i0, i1, op)

        def ts1(o, i0, s1, op0):
            return lambda e, o=o, i0=i0, s1=s1, op0=op0: e.tensor_single_scalar(o, i0, s1, op0)

        def ts2(o, i0, s1, s2, op0, op1):
            return lambda e, o=o, i0=i0, s1=s1, s2=s2, op0=op0, op1=op1: e.tensor_scalar(o, i0, s1, s2, op0, op1)

        def stt(o, i0, s, i1, op0, op1):
            return lambda e, o=o, i0=i0, s=s, i1=i1, op0=op0, op1=op1: e.scalar_tensor_tensor(o, i0, s, i1, op0, op1)

        def cpy(o, i):
            def f(e, o=o, i=i):
                if hasattr(e, "tensor_copy"):
                    return e.tensor_copy(o, i)
                return e.copy(o, i)
            return f

        def sel(o, m, a, z):
            return lambda e, o=o, m=m, a=a, z=z: e.select(o, m, a, z)

        def mm(o, lt, r, st, sp):
            return lambda e, o=o, lt=lt, r=r, st=st, sp=sp: e.matmul(o, lt, r, start=st, stop=sp)

        def trp(o, i, idn):
            return lambda e, o=o, i=i, idn=idn: e.transpose(o, i, idn)

        def dma(o, i):
            return lambda e, o=o, i=i: e.dma_start(o, i)

        def mset(apx, v):
            return lambda e, apx=apx, v=v: e.memset(apx, v)

        def ig(o, i, idxap, bc=None):
            return lambda e, o=o, i=i, idxap=idxap, bc=bc: e.indirect_dma_start(
                o, None, i, IndirectOffsetOnAxis(idxap, 0),
                bounds_check=bc, oob_is_err=False)

        def isca(o, idxap, i, bc):
            return lambda e, o=o, idxap=idxap, i=i, bc=bc: e.indirect_dma_start(
                o, IndirectOffsetOnAxis(idxap, 0), i, None,
                bounds_check=bc, oob_is_err=False, compute_op=ALU.add)

        def rsum(o, i):
            return lambda e, o=o, i=i: e.reduce_sum(o, i, AX.X, None)

        ev_c = []
        for (dst, src) in ((s_identb[:], identb), (s_identf[:], identf),
                           (s_convw[:], convwT), (s_convb[:], convbT),
                           (s_dtb[:], dtbT), (s_dvec[:], dvecT), (s_A2[:], alogT),
                           (s_flagc[:], flagcol), (s_eid[:], eidf), (s_wT[:], wTd),
                           (s_tok[:], tokT), (s_kof[:], kofT), (s_ag2i[:], ag2idx)):
            ev_c.append(b.on("y", dma(dst, src), sig="CST"))
        ev_consts = ev_c[-1]
        ev_ipc = b.on("g", dma(ipWT_b, ipwT), sig="WCA")
        ev_opc = b.on("g", dma(opWT_b, opwT), sig="WCA")
        ev_xpc = b.on("g", lambda e: e.dma_start(
            s_xpw[:, :, :], xpwT.rearrange("(k p) c -> p k c", p=128)), sig="WCA")
        ev_dtc = b.on("g", dma(s_dtw[:], dtwT), sig="WCA")
        ev_a2 = b.on("s", act(s_A2[:], s_A2[:], AF.Exp), waits=[ev_consts], sig="S")
        ev_a2 = b.on("v", ts1(s_A2[:], s_A2[:], -1.0, ALU.mult), waits=[ev_a2], sig="V")
        ev_z = b.on("v", mset(s_zeroI[:], 0), sig="V")
        b.on("v", mset(s_fbig[:], FBIG))
        ev_hb = b.on("g", ig(s_hbias[:], bias_d, s_zeroI[:]), waits=[ev_z], sig="HBG")
        ev_hb = b.on("v", ts1(s_hbias[:], s_hbias[:], 0.5, ALU.mult), waits=[ev_hb], sig="V")
        ev_iotaR = b.on("g", ig(s_iotaR[:], iota256, s_zeroI[:]), waits=[ev_z], sig="HBG")

        def ln_rows(src_ap, out_bf, scratch, col, waits):
            w = list(waits)
            b.on("v", rsum(col[:, 0:1], src_ap), waits=w)
            evq = b.on("s", act(scratch, src_ap, AF.Square), waits=w, sig="S")
            b.on("v", rsum(col[:, 1:2], scratch), waits=[evq])
            b.on("v", ts1(col[:, 0:1], col[:, 0:1], 1.0 / DM, ALU.mult))
            b.on("v", ts1(col[:, 1:2], col[:, 1:2], 1.0 / DM, ALU.mult))
            b.on("v", tt(col[:, 2:3], col[:, 0:1], col[:, 0:1], ALU.mult))
            b.on("v", tt(col[:, 1:2], col[:, 1:2], col[:, 2:3], ALU.subtract))
            ev = b.on("v", ts1(col[:, 1:2], col[:, 1:2], 1e-5, ALU.add), sig="V")
            ev = b.on("s", act(col[:, 3:4], col[:, 1:2], AF.Ln), waits=[ev], sig="S")
            ev = b.on("v", ts1(col[:, 3:4], col[:, 3:4], -0.5, ALU.mult), waits=[ev], sig="V")
            ev = b.on("s", act(col[:, 3:4], col[:, 3:4], AF.Exp), waits=[ev], sig="S")
            ev = b.on("v", ts2(out_bf, src_ap, col[:, 0:1], col[:, 3:4],
                               ALU.subtract, ALU.mult), waits=[ev], sig="V")
            return ev

        st_moe = {}

        def emit_moe(tag, src_tiles, fence, do_routing=True):
            with contextlib.ExitStack() as ar:
                sba = lambda n, sh, dt=F32: ar.enter_context(nc.sbuf_tensor(tag + n, list(sh), dt))
                psa = lambda n, sh, dt=F32: ar.enter_context(nc.psum_tensor(tag + n, list(sh), dt))
                m_idxb = sba("idxb", (8, 2 * TOKS), I32)
                m_idxf = sba("idxf", (8, 2 * TOKS), F32)
                m_mask = sba("mask", (8, 2 * TOKS), F32)
                m_maskI = sba("maskI", (8, 2 * TOKS), I32)
                m_zero8 = sba("zero8", (8, 2 * TOKS), F32)
                m_rank = sba("rank", (8, 2 * TOKS), F32)
                m_nbig8 = sba("nbig8", (8, 2 * TOKS), F32)
                m_rkT = sba("rkT", (128, 64), F32)
                m_vals = sba("vals", (128, 8 * 4), F32)
                m_oh = sba("oh", (128, 2, 128), F32)
                m_gF = sba("gF", (128, 16), F32)
                m_aF = sba("aF", (128, 16), F32)
                m_bF = sba("bF", (128, 16), F32)
                m_zcol = sba("zcol", (128, 1), F32)
                m_dcol = sba("dcol", (128, 1), F32)
                m_mk = sba("mk", (128, 1), F32)
                m_mkI = sba("mkI", (128, 2), I32)
                m_lnsc = sba("lnsc", (128, DM), F32)
                m_lncol = sba("lncol", (128, 8), F32)
                m_lnb = sba("lnb", (128, DM), BF16)
                m_pre = sba("pre", (128, DM), F32)
                m_XG = sba("XG", (128, 2, DM), BF16)
                m_XT = sba("XT", (128, 2, 8 * CAPL), BF16)
                m_hT = sba("hT", (128, 2, 32 * CAPL), BF16)
                m_w1s = sba("w1s", (128, 4, 512), BF16)
                m_w2s = sba("w2s", (128, 4, 512), BF16)
                m_yT = sba("yT", (128, 8 * CAPL), F32)
                m_yr = sba("yr", (128, 2, DM), F32)
                m_yrW = sba("yrW", (128, 2, DM), F32)
                p_rt = psa("prt", (128, 128), F32)
                p_g1t = [psa("pg1%d" % i, (128, 128), BF16) for i in range(2)]
                p_w1t = [psa("pw1%d" % i, (128, 2 * CAPL), F32) for i in range(2)]
                p_w1 = [p_w1t[i // 2][:, (i % 2) * CAPL:(i % 2 + 1) * CAPL] for i in range(4)]
                p_w2t = [psa("pw2%d" % i, (128, 2 * CAPL), F32) for i in range(2)]
                p_w2 = [p_w2t[i // 2][:, (i % 2) * CAPL:(i % 2 + 1) * CAPL] for i in range(4)]
                p_yt = psa("pyt", (128, 512), F32)

                if do_routing:
                    ev_idx = b.on("g", ig(m_idxb[:], idxflat, s_zeroI[0:8, :]),
                                  waits=[ev_z] + fence, sig=tag + "IDX")
                    ev_idx = b.on("v", cpy(m_idxf[:], m_idxb[:]), waits=[ev_idx] + fence, sig="V")
                    b.on("v", mset(m_zero8[:], 0.0), waits=fence)
                    b.on("v", mset(m_nbig8[:], -FBIG))
                    b.on("v", ts1(m_mask[:], m_idxf[:], s_eid[:, 0:1], ALU.is_equal),
                         waits=[ev_consts, ev_idx])
                    b.on("v", cpy(m_maskI[:], m_mask[:]))
                    b.on("v", lambda e: e.tensor_tensor_scan(
                        m_rank[:], m_mask[:], m_zero8[:], 0.0, ALU.add, ALU.add))
                    ev_rs = b.on("v", sel(m_idxf[:], m_maskI[:], m_rank[:], m_nbig8[:]), sig="V")
                    ev_tr_prev = None
                    evs_rk = []
                    for jt in range(8):
                        wt = [ev_rs] + ([ev_tr_prev] if ev_tr_prev else []) + fence
                        evt = b.on("t", trp(p_rt[:, 0:8], m_idxf[:, jt * 128:(jt + 1) * 128],
                                            s_identf[0:8, 0:8]), waits=wt, sig="T")
                        ev_tr_prev = b.on("s", cpy(m_rkT[:, jt * 8:(jt + 1) * 8], p_rt[:, 0:8]),
                                          waits=[evt] + fence, sig="S")
                        evs_rk.append(ev_tr_prev)
                    ev_rkm1 = b.on("v", ts1(m_rkT[:], m_rkT[:], 1.0, ALU.subtract),
                                   waits=evs_rk, sig="V")
                    ev_vals = []
                    for jt in range(8):
                        c0 = jt * 4
                        b.on("v", cpy(m_vals[:, c0:c0 + 1], s_tok[:, jt:jt + 1]),
                             waits=[ev_consts] + fence)
                        b.on("v", ts1(m_vals[:, c0 + 1:c0 + 2], s_wT[:, jt:jt + 1], 0.5, ALU.mult))
                        b.on("v", cpy(m_vals[:, c0 + 2:c0 + 3], s_kof[:, jt:jt + 1]))
                        ev_vals.append(b.on("v", mset(m_vals[:, c0 + 3:c0 + 4], 1.0), sig="V"))
                    ev_rv = []
                    ev_mm_jt = {}
                    for col in range(16):
                        ee, rt = col // 2, col % 2
                        for jt in range(8):
                            woh = [ev_rkm1, ev_iotaR] + ev_vals
                            key = (col, jt - 2) if jt >= 2 else (col - 1, 6 + jt)
                            if key in ev_mm_jt:
                                woh.append(ev_mm_jt[key])
                            evo = b.on("v", ts1(m_oh[:, jt % 2, :],
                                                s_iotaR[:, rt * 128:(rt + 1) * 128],
                                                m_rkT[:, jt * 8 + ee:jt * 8 + ee + 1],
                                                ALU.is_equal), waits=woh, sig="V")
                            wmm = [evo]
                            if jt == 0 and col > 0:
                                wmm.append(ev_rv[col - 1])
                            ev_mm_jt[(col, jt)] = b.on(
                                "t", mm(p_rt[:, 8:12], m_oh[:, jt % 2, :],
                                        m_vals[:, jt * 4:(jt + 1) * 4], jt == 0, jt == 7),
                                waits=wmm, sig="T")
                        ev_rv.append(b.on("s", cpy(m_rv[:, col * 4:(col + 1) * 4], p_rt[:, 8:12]),
                                          waits=[ev_mm_jt[(col, 7)]], sig="S"))
                    ev_route = []
                    b.on("v", mset(m_zcol[:], 0.0), waits=fence)
                    b.on("v", mset(m_dcol[:], float(TOKS)))
                    for col in range(16):
                        ctok = m_rv[:, col * 4:col * 4 + 1]
                        ckp = m_rv[:, col * 4 + 2:col * 4 + 3]
                        cnt = m_rv[:, col * 4 + 3:col * 4 + 4]
                        w = [ev_rv[col]]
                        b.on("v", cpy(m_mkI[:, 0:1], cnt), waits=w)
                        b.on("v", sel(m_gF[:, col:col + 1], m_mkI[:, 0:1], ctok, m_zcol[:, 0:1]))
                        b.on("v", ts1(m_mk[:, 0:1], ckp, 0.5, ALU.is_lt), waits=w)
                        b.on("v", tt(m_mk[:, 0:1], m_mk[:, 0:1], cnt, ALU.mult))
                        b.on("v", cpy(m_mkI[:, 1:2], m_mk[:, 0:1]))
                        b.on("v", sel(m_aF[:, col:col + 1], m_mkI[:, 1:2], ctok, m_dcol[:, 0:1]))
                        b.on("v", ts1(m_mk[:, 0:1], ckp, 0.5, ALU.is_gt), waits=w)
                        b.on("v", tt(m_mk[:, 0:1], m_mk[:, 0:1], cnt, ALU.mult))
                        b.on("v", cpy(m_mkI[:, 1:2], m_mk[:, 0:1]))
                        ev_route.append(b.on("v", sel(m_bF[:, col:col + 1], m_mkI[:, 1:2], ctok,
                                                      m_dcol[:, 0:1]), sig="V"))
                    b.on("v", cpy(m_gI[:], m_gF[:]), waits=ev_route)
                    b.on("v", cpy(m_aI[:], m_aF[:]))
                    ev_gI = b.on("v", cpy(m_bI[:], m_bF[:]), sig="V")

                    st_moe["ev_rv"] = ev_rv[-1]
                    st_moe["ev_gI"] = ev_gI
                    ev_rv_last = ev_rv[-1]
                else:
                    ev_rv_last = st_moe["ev_rv"]
                    ev_gI = st_moe["ev_gI"]
                ev_lnd = []
                ev_pre = []
                ev_lnw = None
                for rt4 in range(4):
                    apx, rdy = src_tiles[rt4]
                    w = [rdy] + fence + ([ev_lnw] if ev_lnw else [])
                    evr = ln_rows(apx, m_lnb[:], m_lnsc[:], m_lncol[:], w)
                    ev_lnw = b.on("y", dma(lnrows_d[rt4 * 128:(rt4 + 1) * 128, :], m_lnb[:]),
                                  waits=[evr], sig=tag + "LNR")
                    ev_lnd.append(ev_lnw)
                    evp = b.on("v", tt(m_pre[:], apx, s_hbias[:], ALU.add),
                               waits=[rdy, ev_hb] + ([ev_pre[-1]] if ev_pre else []), sig="V")
                    ev_pre.append(b.on("y", dma(outrows_d[rt4 * 128:(rt4 + 1) * 128, :],
                                                m_pre[:]), waits=[evp], sig=tag + "PRE"))
                    if rt4 == 3:
                        ev_pre.append(b.on("y", dma(outrows_d[TOKS:TOKS + 1, :],
                                                    m_pre[0:1, :]),
                                           waits=[evp], sig=tag + "PRE"))

                if DEBUG_STOP == "pre":
                    return ev_pre[-1]
                ev_sc_last = None
                ev_w1done = {0: None, 1: None}
                w1ring_ev = [None] * 4
                w2ring_ev = [None] * 4
                ev_yr_used = {0: None, 1: None}
                ev_yt_last = None
                ev_yrp_roll = []
                gath_tail = []
                for ee in range(DEBUG_E or E):
                    pe = ee % 2
                    ev_gath = list(gath_tail)
                    for rt in range(2):
                        col = ee * 2 + rt
                        wm = ([ev_gI] + ([ev_w1done[pe]] if ev_w1done[pe] else []) +
                              ([ev_gath[-1]] if ev_gath else []) + fence)
                        evg = b.on("g", ig(m_XG[:, pe, :], lnrows_d,
                                           m_gI[:, col:col + 1]),
                                   waits=wm + ev_lnd, sig=tag + "XG" + str(pe))
                        for kb in range(8):
                            evt = b.on("t", trp(p_g1t[kb % 2][:],
                                                m_XG[:, pe, kb * 128:(kb + 1) * 128],
                                                s_identb[:]),
                                       waits=[evg] + ([ev_gath[-2]] if len(ev_gath) >= 2
                                                      else []), sig="T")
                            w_ = 128 if rt == 0 else CAPL - 128
                            ev_gath.append(b.on("s", cpy(
                                m_XT[:, pe, kb * CAPL + rt * 128:
                                     kb * CAPL + rt * 128 + w_],
                                p_g1t[kb % 2][:, 0:w_]),
                                waits=[evt], sig="S"))
                    gath_tail = ev_gath[-2:]
                    ev_h = []
                    nw1 = (DEBUG_MG or 8) * 8
                    w1ld = {}

                    def w1load(j, extra=()):
                        ring = j % 4
                        mg_, k_ = j // 8, j % 8
                        w1ld[j] = b.on("g", dma(
                            m_w1s[:, ring, :],
                            w1_d[ee, k_ * 128:(k_ + 1) * 128, mg_ * 512:(mg_ + 1) * 512]),
                            waits=list(extra) +
                                  ([w1ring_ev[ring]] if w1ring_ev[ring] else []),
                            sig=tag + "W1" + str(ring))
                    for j in range(min(4, nw1)):
                        w1load(j)
                    for mg in range(DEBUG_MG or 8):
                        for k in range(8):
                            j = mg * 8 + k
                            ring = j % 4
                            for ml in range(4):
                                ev_mm = b.on("t", mm(
                                    p_w1[ml], m_w1s[:, ring, ml * 128:(ml + 1) * 128],
                                    m_XT[:, pe, k * CAPL:(k + 1) * CAPL],
                                    k == 0 and ml % 2 == 0, k == 7 and ml % 2 == 1),
                                    waits=[w1ld[j]] + (ev_gath if k == 0 else []) +
                                          (ev_h[-4:] if (k == 0 and ev_h) else []),
                                    sig="T" if ml == 3 else None)
                            w1ring_ev[ring] = ev_mm
                            if j + 4 < nw1:
                                w1load(j + 4)
                        for ml in range(4):
                            mt = mg * 4 + ml
                            ev_h.append(b.on("s", act(m_hT[:, pe, mt * CAPL:(mt + 1) * CAPL],
                                                      p_w1[ml], AF.Gelu),
                                             waits=[ev_mm], sig="S"))
                    ev_w1done[pe] = ev_h[-1]
                    if DEBUG_MG:
                        return ev_h[-1]
                    ev_ye = []
                    w2ld = {}

                    def w2load(j):
                        ring = j % 4
                        mg_, k_ = j // 32, j % 32
                        w2ld[j] = b.on("g", dma(
                            m_w2s[:, ring, :],
                            w2_d[ee, k_ * 128:(k_ + 1) * 128, mg_ * 512:(mg_ + 1) * 512]),
                            waits=[w2ring_ev[ring]] if w2ring_ev[ring] else [],
                            sig=tag + "W2" + str(ring))
                    for j in range(4):
                        w2load(j)
                    for mg in range(2):
                        for k in range(32):
                            j = mg * 32 + k
                            ring = j % 4
                            for ml in range(4):
                                ev_mm = b.on("t", mm(
                                    p_w2[ml], m_w2s[:, ring, ml * 128:(ml + 1) * 128],
                                    m_hT[:, pe, k * CAPL:(k + 1) * CAPL],
                                    k == 0 and ml % 2 == 0, k == 31 and ml % 2 == 1),
                                    waits=[w2ld[j]] + (ev_h if k == 0 else []) +
                                          (ev_ye[-4:] if (k == 0 and ev_ye) else []),
                                    sig="T" if ml == 3 else None)
                            w2ring_ev[ring] = ev_mm
                            if j + 4 < 64:
                                w2load(j + 4)
                        for ml in range(4):
                            mt = mg * 4 + ml
                            ev_ye.append(b.on("s", act(m_yT[:, mt * CAPL:(mt + 1) * CAPL],
                                                       p_w2[ml], AF.Identity),
                                              waits=[ev_mm] +
                                                    ([ev_yt_last] if ev_yt_last else []),
                                              sig="S"))
                    for rt in range(2):
                        col = ee * 2 + rt
                        ev_yrp = []
                        w_ = 128 if rt == 0 else CAPL - 128
                        for ml in range(8):
                            pbuf = (p_yt[0:w_, 0:128] if ml % 2 == 0
                                    else p_rt[0:w_, :])
                            evt = b.on("t", trp(pbuf,
                                                m_yT[:, ml * CAPL + rt * 128:
                                                     ml * CAPL + rt * 128 + w_],
                                                s_identf[:]),
                                       waits=ev_ye + [ev_rv_last] +
                                             ([ev_yr_used[rt]] if ev_yr_used[rt] else []) +
                                             ([ev_yrp_roll[-2]] if len(ev_yrp_roll) >= 2
                                              else []),
                                       sig="T")
                            ev_yt_last = evt
                            ev = b.on("s", cpy(
                                m_yr[0:w_, rt, ml * 128:(ml + 1) * 128],
                                pbuf),
                                waits=[evt], sig="S")
                            ev_yrp.append(ev)
                            ev_yrp_roll.append(ev)
                        cw = m_rv[:, col * 4 + 1:col * 4 + 2]
                        evsA = b.on("v", ts1(m_yrW[0:w_, 0, :], m_yr[0:w_, rt, :],
                                             cw[0:w_, :], ALU.mult),
                                    waits=ev_yrp + [ev_gI] +
                                          ([ev_sc_last] if ev_sc_last else []), sig="V")
                        if w_ < 128:
                            evsA = b.on("v", mset(m_yrW[64:128, 0, :], 0.0),
                                        waits=[ev_sc_last] if ev_sc_last else [],
                                        sig="V")
                        wA = [evsA] + ev_pre + ([ev_sc_last] if ev_sc_last else [])
                        evA = b.on("g", isca(outrows_d, m_aI[:, col:col + 1],
                                             m_yrW[:, 0, :], None), waits=wA, sig=tag + "SC")
                        ev_sc_last = b.on("g", isca(outrows_d, m_bI[:, col:col + 1],
                                                    m_yrW[:, 0, :], None),
                                          waits=[evA], sig=tag + "SC")
                        ev_yr_used[rt] = ev_sc_last
                return ev_sc_last

        with contextlib.ExitStack() as s1s:
            s1_x0 = s1s.enter_context(nc.sbuf_tensor("s1_x0", [128, 4, DM], F32))
            s1_tiles = []
            for rt4 in range(4):
                ev = b.on("y", dma(s1_x0[:, rt4, :], x0s[rt4 * 128:(rt4 + 1) * 128, :]),
                          sig="S1X")
                s1_tiles.append((s1_x0[:, rt4, :], ev))
            ev_moe1 = emit_moe("m1", s1_tiles, [])
            if DEBUG_STOP is None and not (DEBUG_E or DEBUG_MG):
                ev_ag1 = b.on("g", lambda e: e.collective_compute(
                    "AllGather", ALU.bypass, replica_groups=[list(range(NCORE))],
                    ins=[outrows_d[0:TOKS, :].opt()], outs=[ag1out.opt()]),
                    waits=[ev_moe1], sig="CC")

        if parts == "s1":
            block.vector(lambda e: b.play("v", e, sems))
            block.scalar(lambda e: b.play("s", e, sems))
            block.tensor(lambda e: b.play("t", e, sems))
            block.gpsimd(lambda e: b.play("g", e, sems))
            block.sync(lambda e: b.play("y", e, sems))
            es.pop_all().close()
            return nc

        fence1 = ([b.now("V"), b.now("S"), b.now("T"), b.now("G"), b.now("S1X")] +
                  [b.now("m1" + k) for k in ("IDX", "LNR", "PRE", "XG0", "XG1", "SC",
                                             "W10", "W11", "W12", "W13",
                                             "W20", "W21", "W22", "W23")])

        with contextlib.ExitStack() as ma:
            sba = lambda n, sh, dt=F32: ma.enter_context(nc.sbuf_tensor(n, list(sh), dt))
            psa = lambda n, sh, dt=F32: ma.enter_context(nc.psum_tensor(n, list(sh), dt))
            a_x1 = sba("a_x1", (128, 2, 2, DM), F32)
            a_lnsc = sba("a_lnsc", (128, DM), F32)
            a_lncol = sba("a_lncol", (128, 8), F32)
            a_lnb = sba("a_lnb", (128, DM), BF16)
            a_lnT = sba("a_lnT", (128, 8, TC), BF16)
            a_ipw = sba("a_ipw", (128, 4, 512), BF16)
            a_xse = sba("a_xse", (128, DTI * (TC + 3)), BF16)
            a_cs = sba("a_cs", (128, 2, TC), F32)
            a_ub = sba("a_ub", (128, DTI * TC), BF16)
            a_uD = sba("a_uD", (128, 2, DTI * TC), BF16)
            a_g = sba("a_g", (128, 2, DTI * TC), BF16)
            a_dtT = sba("a_dtT", (DR, TC), BF16)
            a_bc = sba("a_bc", (32, TC), BF16)
            a_sp = sba("a_sp", (128, TC), F32)
            a_dl = sba("a_dl", (128, 2, DTI * TC), BF16)
            a_du = sba("a_du", (128, 2, DTI * TC), BF16)
            a_Bb = sba("a_Bb", (128, DS * TC), BF16)
            a_Cb = sba("a_Cb", (128, DS * TC), BF16)
            a_dA = sba("a_dA", (128, 4, 4 * TC), BF16)
            a_dBu = sba("a_dBu", (128, 2, 4 * TC), BF16)
            a_h = sba("a_h", (128, 2, 4 * TC), BF16)
            a_hC = sba("a_hC", (128, 2, 8 * TC), BF16)
            a_yp = sba("a_yp", (128, 2, TC), F32)
            a_yp2 = sba("a_yp2", (128, 2, TC), F32)
            a_r1 = sba("a_r1", (128, 2, 4 * TC), BF16)
            a_r2 = sba("a_r2", (128, 2, 2 * TC), BF16)
            a_yf = sba("a_yf", (128, 2, DTI * TC), BF16)
            a_opw = sba("a_opw", (128, 2, 2, 128), BF16)
            a_yo = sba("a_yo", (128, 2, 2 * TC), F32)
            a_x2q = sba("a_x2q", (128, 2, 256), F32)
            a_outq = sba("a_outq", (128, 2, 256), F32)
            p_lnt = [psa("p_ln%d" % i, (128, 128), BF16) for i in range(2)]
            p_ipt = [psa("p_ip%d" % i, (128, 2 * TC), F32) for i in range(2)]
            p_ip = [p_ipt[i // 2][:, (i % 2) * TC:(i % 2 + 1) * TC] for i in range(4)]
            p_xd = psa("p_xd", (128, 2 * TC), F32)[:, 0:TC]
            p_opt = psa("p_op", (128, 2 * TC), F32)
            p_op = [p_opt[:, 0:TC], p_opt[:, TC:2 * TC]]
            p_yot = [psa("p_yo%d" % i, (128, 128), F32) for i in range(2)]

            st = {}
            st["SDv"] = {}; st["SDg"] = {}; st["TDt"] = {}; st["TDv"] = {}
            st["TDs"] = {}; st["DMAq"] = {0: None, 1: None}
            st["lnw"] = None; st["lnT_used"] = None
            st["ipring"] = [None] * 4
            st["opring"] = [None, None]
            st["dA_guard"] = [None] * 4
            st["dB_guard"] = [None, None]
            st["hC_chain"] = [None] * 2
            st["r1_used"] = [None, None]
            st["yp_used"] = [None, None]
            st["yo_used"] = [None, None]
            st["bc_used"] = None
            st["Cb_prev"] = None
            st["blend_prev"] = {0: None, 1: None}
            st["rt_roll"] = []
            st["res_roll"] = []
            st["pdv_prev"] = None
            st["dl_last_prev"] = None
            st["lastip"] = None
            st["lnsc_free"] = None
            st["cs_used"] = [None, None]
            st["xp_done"] = None
            st["x1ev"] = {}
            st["uev"] = {}
            st["gev"] = {}
            st["dlev"] = {}
            st["duev"] = {}
            st["Bbev"] = {}
            st["Cbev"] = {}
            st["yfev"] = {}
            st["x2dma"] = {0: None, 1: None}
            st["outdma"] = {0: None, 1: None}

            def make_head_pieces(c):
                gen = c % 2
                x1ev = {}
                st["x1ev"][c] = x1ev

                def piece_rt(rt):
                    wprev = fence1 if c < 2 else [st["SDv"][c - 2], st["SDg"][c - 2],
                                                  st["TDt"][c - 2], st["TDv"][c - 2],
                                                  st["TDs"][c - 2]]
                    r0 = c * TC + rt * 128
                    evx = b.on("y", dma(a_x1[:, gen, rt, :], myx[r0:r0 + 128, :]),
                               waits=wprev, sig="X1L" + str(rt))
                    eva = b.on("y", dma(a_lnsc[:], ag1out[r0:r0 + 128, :]),
                               waits=[ev_ag1] + ([st["lnsc_free"]] if st["lnsc_free"]
                                                 else []), sig="AGL" + str(rt))
                    ev1 = b.on("v", tt(a_x1[:, gen, rt, :], a_x1[:, gen, rt, :],
                                       s_hbias[:], ALU.add),
                               waits=[evx, ev_hb] + wprev, sig="V", chain=False)
                    ev1 = b.on("v", tt(a_lnsc[:], a_lnsc[:], a_x1[:, gen, rt, :],
                                       ALU.subtract), waits=[ev1, eva], sig="V")
                    ev1 = b.on("v", stt(a_x1[:, gen, rt, :], a_lnsc[:], s_flagc[:, 0:1],
                                        a_x1[:, gen, rt, :], ALU.mult, ALU.add),
                               waits=[ev1, ev_consts], sig="V")
                    x1ev[rt] = ev1
                    st["blend_prev"][rt] = ev1
                    wln = [ev1] + ([st["lnw"]] if st["lnw"] else [])
                    evl = ln_rows(a_x1[:, gen, rt, :], a_lnb[:], a_lnsc[:], a_lncol[:], wln)
                    st["lnsc_free"] = b.now("V")
                    for kb in range(8):
                        evt = b.on("t", trp(p_lnt[kb % 2][:],
                                            a_lnb[:, kb * 128:(kb + 1) * 128], s_identb[:]),
                                   waits=[evl] + ([st["rt_roll"][-2]]
                                                  if len(st["rt_roll"]) >= 2 else []),
                                   sig="T")
                        st["rt_roll"].append(b.on("s", cpy(
                            a_lnT[:, kb, rt * 128:rt * 128 + 128],
                            p_lnt[kb % 2][:]),
                            waits=[evt] + ([st["lnT_used"]] if st["lnT_used"] else []),
                            sig="S"))
                    st["lnw"] = st["rt_roll"][-1]

                return [lambda rt=rt: piece_rt(rt) for rt in range(2)]

            def make_prod_pieces(c):
                gen = c % 2

                def wpf():
                    return fence1 if c < 2 else [st["SDv"][c - 2], st["SDg"][c - 2],
                                                 st["TDt"][c - 2], st["TDv"][c - 2],
                                                 st["TDs"][c - 2]]
                uev = {}
                gev = {}
                dlev = {}
                duev = {}
                st["uev"][c] = uev
                st["gev"][c] = gev
                st["dlev"][c] = dlev
                st["duev"][c] = duev
                ev_ipev = []
                ipld = {}
                box = {}

                def ipload(eg_, k_):
                    ring = k_ % 4
                    ipld[(eg_, k_)] = b.on("y", dma(
                        a_ipw[:, ring, :],
                        ipWT_b[k_ * 128:(k_ + 1) * 128, eg_ * 512:(eg_ + 1) * 512]),
                        waits=[ev_ipc] + ([st["ipring"][ring]] if st["ipring"][ring]
                                          else []), sig="IP" + str(ring))

                def piece_eg(eg):
                    if eg == 0:
                        for j0 in range(4):
                            ipload(0, j0)
                    ev_rt = st["rt_roll"][-16:]
                    for k in range(8):
                        ring = k % 4
                        for ml in range(4):
                            ev_mm = b.on("t", mm(
                                p_ip[ml], a_ipw[:, ring, ml * 128:(ml + 1) * 128],
                                a_lnT[:, k, :],
                                k == 0 and ml % 2 == 0, k == 7 and ml % 2 == 1),
                                waits=[ipld[(eg, k)]] + (ev_rt if k == 0 else []) +
                                      (ev_ipev[-4:] if (k == 0 and eg > 0) else []) +
                                      ([st["lastip"]] if (k == 0 and eg == 0 and
                                                          st["lastip"]) else []),
                                sig="T" if ml == 3 else None)
                        st["ipring"][ring] = ev_mm
                        nj = eg * 8 + k + 4
                        if nj < 64:
                            ipload(nj // 8, nj % 8)
                    evt = ev_mm
                    for ml in range(4):
                        dt = eg * 4 + ml
                        if dt < DTI:
                            x0 = dt * (TC + 3)
                            u0 = dt * TC
                            cp = dt % 2
                            evc = b.on("s", cpy(a_xse[:, x0 + 3:x0 + 3 + TC], p_ip[ml]),
                                       waits=[evt] + ([st["pdv_prev"]] if st["pdv_prev"]
                                                      else []), sig="S")
                            ev_ipev.append(evc)
                            if c == 0:
                                evtl = b.on("v", mset(a_xse[:, x0:x0 + 3], 0.0),
                                            waits=[st["pdv_prev"]] if st["pdv_prev"]
                                            else [], sig="V", chain=False)
                            else:
                                evtl = b.on("v", cpy(a_xse[:, x0:x0 + 3],
                                                     s_tail[:, dt * 3:dt * 3 + 3]),
                                            sig="V", chain=False)
                            b.on("v", cpy(s_tail[:, dt * 3:dt * 3 + 3],
                                          a_xse[:, x0 + TC:x0 + TC + 3]), waits=[evc],
                                 chain=False)
                            b.on("v", ts1(a_cs[:, cp, :], a_xse[:, x0 + 3:x0 + 3 + TC],
                                          s_convw[:, dt * 4 + 3:dt * 4 + 4], ALU.mult),
                                 waits=[evc, evtl] + ([st["cs_used"][cp]]
                                                      if st["cs_used"][cp] else []),
                                 chain=False)
                            for j in (2, 1):
                                b.on("v", stt(a_cs[:, cp, :], a_xse[:, x0 + j:x0 + j + TC],
                                              s_convw[:, dt * 4 + j:dt * 4 + j + 1],
                                              a_cs[:, cp, :], ALU.mult, ALU.add))
                            evcv = b.on("v", stt(a_cs[:, cp, :], a_xse[:, x0:x0 + TC],
                                                 s_convw[:, dt * 4:dt * 4 + 1],
                                                 a_cs[:, cp, :], ALU.mult, ALU.add),
                                        sig="V")
                            evsl = b.on("s", act(a_ub[:, u0:u0 + TC], a_cs[:, cp, :],
                                                 AF.Silu, bias=s_convb[:, dt:dt + 1]),
                                        waits=[evcv, ev_consts] +
                                              ([st["xp_done"]] if st["xp_done"] else []),
                                        sig="S")
                            st["cs_used"][cp] = evsl
                            evud = b.on("v", ts1(a_uD[:, gen, u0:u0 + TC],
                                                 a_ub[:, u0:u0 + TC],
                                                 s_dvec[:, dt:dt + 1], ALU.mult),
                                        waits=[evsl] + wpf(), sig="V", chain=False)
                            uev[dt] = (evsl, evud)
                        else:
                            dr = dt - DTI
                            u0 = dr * TC
                            gev[dr] = b.on("s", act(a_g[:, gen, u0:u0 + TC], p_ip[ml],
                                                    AF.Silu), waits=[evt] + wpf(), sig="S")
                            ev_ipev.append(gev[dr])
                    if eg == 7:
                        st["lastip"] = b.now("S")
                        st["lnT_used"] = b.now("T")

                def piece_xp():
                    for k in range(DTI):
                        ev_mm = b.on("t", mm(p_xd[0:96], s_xpw[:, k, :],
                                             a_ub[:, k * TC:(k + 1) * TC],
                                             k == 0, k == DTI - 1),
                                     waits=[ev_xpc, uev[k][0]] +
                                           ([st["bc_used"]] if (k == 0 and st["bc_used"])
                                            else []) +
                                           ([st["dl_last_prev"]] if (k == 0 and
                                                                     st["dl_last_prev"])
                                            else []),
                                     sig="T" if k == DTI - 1 else None)
                    evt = ev_mm
                    st["xp_done"] = evt
                    ev_dtT = b.on("s", cpy(a_dtT[:, :], p_xd[0:64]), waits=[evt], sig="S")
                    ev_bcv = b.on("s", cpy(a_bc[:, :], p_xd[64:96]), waits=[evt], sig="S")
                    wbc = [ev_bcv] + ([st["Cb_prev"]] if st["Cb_prev"] else [])
                    b.on("y", dma(bcB_d, a_bc[0:16, :]), waits=wbc, sig="BCD")
                    ev_bcd = b.on("y", dma(bcC_d, a_bc[16:32, :]), sig="BCD")
                    st["bc_used"] = ev_bcd
                    box["bcd"] = ev_bcd
                    box["dtT"] = ev_dtT
                    box["bcv"] = ev_bcv

                def piece_dt(q):
                    prev_dl = box.get("prev_dl")
                    for mt in range(q * 4, q * 4 + 4):
                        evmm = b.on("t", mm(p_xd, s_dtw[:, mt * 128:(mt + 1) * 128],
                                            a_dtT[:, :], True, True),
                                    waits=[ev_dtc, box["dtT"], box["bcv"]] +
                                          ([prev_dl] if prev_dl else []), sig="T")
                        d0 = mt * TC
                        evsp = b.on("s", act(a_sp[:, :], p_xd, AF.Exp,
                                             bias=s_dtb[:, mt:mt + 1]),
                                    waits=[evmm, ev_consts], sig="S")
                        dlev[mt] = b.on("s", act(a_dl[:, gen, d0:d0 + TC], a_sp[:, :],
                                                 AF.Ln, bias=1.0),
                                        waits=[evsp] + wpf(), sig="S")
                        prev_dl = dlev[mt]
                        duev[mt] = b.on("v", tt(a_du[:, gen, d0:d0 + TC],
                                                a_dl[:, gen, d0:d0 + TC],
                                                a_ub[:, d0:d0 + TC], ALU.mult),
                                        waits=[dlev[mt]] + wpf(), sig="V",
                                        chain=False)
                    box["prev_dl"] = prev_dl
                    if q == 3:
                        st["dl_last_prev"] = dlev[DTI - 1]
                        st["pdv_prev"] = b.now("V")

                def piece_gather():
                    st["Bbev"][c] = b.on("g", ig(a_Bb[:, :], bcB_d, s_zeroI[:]),
                                         waits=[box["bcd"], ev_z], sig="BBG")
                    st["Cbev"][c] = b.on("g", ig(a_Cb[:, :], bcC_d, s_zeroI[:]),
                                         waits=[box["bcd"], b.now("V")], sig="CBG")
                    st["Cb_prev"] = st["Cbev"][c]

                return ([lambda eg=eg: piece_eg(eg) for eg in range(8)] +
                        [piece_xp] +
                        [lambda q=q: piece_dt(q) for q in range(4)] +
                        [None, None, piece_gather])

            def emit_scan(c, pieces):
                gen = c % 2
                dlev = st["dlev"][c]
                duev = st["duev"][c]
                uev = st["uev"][c]
                gev = st["gev"][c]
                wtd = [st["TDt"][c - 2]] if c >= 2 else []
                yfev = []

                def stage_a(dt):
                    d0 = dt * TC
                    halves = []
                    for half in range(2):
                        hp = (dt * 2 + half) % 2
                        whc = [st["hC_chain"][hp]] if st["hC_chain"][hp] else []
                        hc_evs = []
                        for gl in range(2):
                            grp = half * 2 + gl
                            pg = grp % 2
                            n0 = grp * 4
                            ev_dA = None
                            for nl in range(4):
                                n = n0 + nl
                                ev_dA = b.on("s", act(
                                    a_dA[:, grp, nl * TC:(nl + 1) * TC],
                                    a_dl[:, gen, d0:d0 + TC], AF.Exp,
                                    scale=s_A2[:, dt * DS + n:dt * DS + n + 1]),
                                    waits=[dlev[dt], ev_a2] +
                                          ([st["dA_guard"][grp]]
                                           if st["dA_guard"][grp] else []),
                                    sig="S" if nl == 3 else None)
                            ev_dB = b.on("g", lambda e, pg=pg, gen=gen, d0=d0, n0=n0:
                                         e.tensor_tensor(
                                             a_dBu[:, pg, :].rearrange(
                                                 "p (n t) -> p n t", n=4),
                                             a_du[:, gen, d0:d0 + TC].rearrange(
                                                 "p (o t) -> p o t", o=1).broadcast_to(
                                                 (128, 4, TC)),
                                             a_Bb[:, n0 * TC:(n0 + 4) * TC].rearrange(
                                                 "p (n t) -> p n t", n=4), ALU.mult),
                                         waits=[duev[dt], st["Bbev"][c]] +
                                               ([st["dB_guard"][pg]]
                                                if st["dB_guard"][pg] else []),
                                         sig="G", chain=False)
                            ev_sc = None
                            for nl in range(4):
                                n = n0 + nl
                                ini = (0.0 if c == 0 else
                                       s_hst[:, dt * DS + n:dt * DS + n + 1])
                                ev_sc = b.on("v", lambda e, grp=grp, pg=pg, nl=nl,
                                             ini=ini:
                                             e.tensor_tensor_scan(
                                                 a_h[:, pg, nl * TC:(nl + 1) * TC],
                                                 a_dA[:, grp, nl * TC:(nl + 1) * TC],
                                                 a_dBu[:, pg, nl * TC:(nl + 1) * TC],
                                                 ini, ALU.mult, ALU.add),
                                             waits=[ev_dA, ev_dB], sig="V",
                                             chain=False)
                            st["dA_guard"][grp] = ev_sc
                            st["dB_guard"][pg] = ev_sc
                            b.on("v", lambda e, pg=pg, dt=dt, n0=n0: e.tensor_copy(
                                s_hst[:, dt * DS + n0:dt * DS + n0 + 4],
                                a_h[:, pg, :].rearrange("p (n t) -> p n t", t=TC)
                                [:, :, TC - 1:TC].rearrange("p n one -> p (n one)")))
                            evhc = b.on("v", lambda e, pg=pg, hp=hp, gl=gl, n0=n0:
                                        e.tensor_tensor(
                                            a_hC[:, hp, gl * 4 * TC:(gl + 1) * 4 * TC],
                                            a_h[:, pg, :],
                                            a_Cb[:, n0 * TC:(n0 + 4) * TC], ALU.mult),
                                        waits=[st["Cbev"][c], ev_sc] + whc, sig="V",
                                        chain=False)
                            hc_evs.append(evhc)
                        halves.append((hp, hc_evs))
                    return halves

                def stage_l1(dt, halves):
                    l1evs = []
                    for half in range(2):
                        hp, hc_evs = halves[half]
                        hp2 = (dt * 2 + half) % 2
                        ev_l1 = b.on("v", tt(a_r1[:, hp2, :],
                                             a_hC[:, hp, 0:4 * TC],
                                             a_hC[:, hp, 4 * TC:8 * TC], ALU.add),
                                     waits=hc_evs + ([st["r1_used"][hp2]]
                                                     if st["r1_used"][hp2] else []),
                                     sig="V", chain=False)
                        st["hC_chain"][hp] = ev_l1
                        l1evs.append(ev_l1)
                    return l1evs

                def stage_tree(dt, l1evs):
                    d0 = dt * TC
                    dp = dt % 2
                    ev_red = []
                    for half in range(2):
                        hp2 = (dt * 2 + half) % 2
                        ypbuf = a_yp if half == 0 else a_yp2
                        ev_l2 = b.on("v", tt(a_r2[:, hp2, :],
                                             a_r1[:, hp2, 0:2 * TC],
                                             a_r1[:, hp2, 2 * TC:4 * TC], ALU.add),
                                     waits=[l1evs[half]], sig="V", chain=False)
                        st["r1_used"][hp2] = ev_l2
                        evrd = b.on("v", tt(ypbuf[:, dp, :],
                                            a_r2[:, hp2, 0:TC],
                                            a_r2[:, hp2, TC:2 * TC], ALU.add),
                                    waits=[st["yp_used"][dp]]
                                          if (half == 0 and st["yp_used"][dp])
                                          else [], sig="V")
                        ev_red.append(evrd)
                    evy = b.on("v", tt(a_yp[:, dp, :], a_yp[:, dp, :],
                                       a_yp2[:, dp, :], ALU.add),
                               waits=ev_red, sig="V", chain=False)
                    evy = b.on("v", tt(a_yp[:, dp, :], a_yp[:, dp, :],
                                       a_uD[:, gen, d0:d0 + TC], ALU.add),
                               waits=[uev[dt][1]], sig="V")
                    evy = b.on("v", tt(a_yf[:, gen, d0:d0 + TC], a_yp[:, dp, :],
                                       a_g[:, gen, d0:d0 + TC], ALU.mult),
                               waits=[gev[dt]] + wtd, sig="V")
                    st["yp_used"][dp] = evy
                    yfev.append(evy)

                prev = None
                for dt in range(DTI):
                    halves = stage_a(dt)
                    if prev is not None:
                        pdt, phalves = prev
                        l1evs = stage_l1(pdt, phalves)
                        stage_tree(pdt, l1evs)
                    prev = (dt, halves)
                    for p in pieces[dt]:
                        p()
                pdt, phalves = prev
                l1evs = stage_l1(pdt, phalves)
                stage_tree(pdt, l1evs)
                st["SDv"][c] = yfev[-1]
                st["SDg"][c] = b.now("G")
                st["yfev"][c] = yfev

            def make_tail_pieces(c):
                gen = c % 2
                yfev = st["yfev"][c]
                x1ev = st["x1ev"][c]
                evs_out = []
                opld = {}

                def opload(hf_, k_):
                    ring = k_ % 2
                    opld[(hf_, k_)] = b.on("y", dma(
                        a_opw[:, ring, :, :].rearrange("p a b -> p (a b)"),
                        opWT_b[k_ * 128:(k_ + 1) * 128, hf_ * 256:(hf_ + 1) * 256]),
                        waits=[ev_opc] + ([st["opring"][ring]] if st["opring"][ring]
                                          else []), sig="OP" + str(ring))

                def piece_hf(hf):
                    if hf == 0:
                        opload(0, 0)
                        opload(0, 1)
                    prev_oev = st.get("oev_prev", [])
                    yp_ = hf % 2
                    ev_oev = []
                    for k in range(DTI):
                        ring = k % 2
                        for ml in range(2):
                            ev_mm = b.on("t", mm(
                                p_op[ml], a_opw[:, ring, ml, :],
                                a_yf[:, gen, k * TC:(k + 1) * TC],
                                k == 0 and ml == 0, k == DTI - 1 and ml == 1),
                                waits=[opld[(hf, k)]] + (yfev if k == 0 else []) +
                                      (prev_oev if k == 0 else []),
                                sig="T" if ml == 1 else None)
                        st["opring"][ring] = ev_mm
                        nj = hf * DTI + k + 2
                        if nj < 4 * DTI:
                            opload(nj // DTI, nj % DTI)
                    evt = ev_mm
                    for ml in range(2):
                        ev_oev.append(b.on("s", act(a_yo[:, yp_, ml * TC:(ml + 1) * TC],
                                                    p_op[ml], AF.Identity),
                                           waits=[evt] + ([st["yo_used"][yp_]]
                                                          if st["yo_used"][yp_] else []),
                                           sig="S"))
                    for rt in range(2):
                        r0 = c * TC + rt * 128
                        qp = (hf * 2 + rt) % 2
                        ev_tr = []
                        for dq in range(2):
                            evtr = b.on("t", trp(p_yot[dq][:],
                                                 a_yo[:, yp_, dq * TC + rt * 128:
                                                      dq * TC + rt * 128 + 128],
                                                 s_identf[:]),
                                        waits=ev_oev + ([st["res_roll"][-2]]
                                                        if len(st["res_roll"]) >= 2
                                                        else []), sig="T")
                            evr = b.on("v", tt(a_x2q[:, qp, dq * 128:(dq + 1) * 128],
                                               p_yot[dq][:],
                                               a_x1[:, gen, rt,
                                                    hf * 256 + dq * 128:
                                                    hf * 256 + (dq + 1) * 128], ALU.add),
                                       waits=[evtr, x1ev[rt]] +
                                             ([st["x2dma"][qp]] if st["x2dma"][qp]
                                              else []), sig="V", chain=False)
                            ev_tr.append(evr)
                            st["res_roll"].append(evr)
                        evo = b.on("v", tt(a_outq[:, qp, :], a_x2q[:, qp, :],
                                           s_hbias[:, hf * 256:(hf + 1) * 256], ALU.add),
                               waits=ev_tr + ([st["outdma"][qp]] if st["outdma"][qp]
                                              else []), sig="V")
                        evd1 = b.on("y", dma(out_my[r0:r0 + 128,
                                                    hf * 256:(hf + 1) * 256],
                                             a_outq[:, qp, :]), waits=[evo], sig="OD" + str(qp))
                        st["outdma"][qp] = evd1
                        evd2 = b.on("y", dma(ag2in[r0:r0 + 128, hf * 256:(hf + 1) * 256],
                                             a_x2q[:, qp, :]), waits=[ev_tr[-1]],
                                    sig="XD" + str(qp))
                        st["x2dma"][qp] = evd2
                        evs_out.extend([evd1, evd2])
                    st["yo_used"][yp_] = b.now("T")
                    st["oev_prev"] = ev_oev
                    if hf == 3:
                        st["TDt"][c] = b.now("T")
                        st["TDv"][c] = b.now("V")
                        st["TDs"][c] = b.now("S")
                        b.on("g", lambda e, c=c: e.collective_compute(
                            "AllGather", ALU.bypass,
                            replica_groups=[list(range(NCORE))],
                            ins=[ag2in[c * TC:(c + 1) * TC, :].opt()],
                            outs=[ag2out[c * NCORE * TC:
                                         (c + 1) * NCORE * TC, :].opt()]),
                            waits=evs_out, sig="CC")
                        st["lastout"] = evs_out[-1]

                return [lambda hf=hf: piece_hf(hf) for hf in range(4)]

            PRODW = ((6, 0), (7, 1), (8, 2), (9, 3), (10, 8), (11, 4), (11, 9),
                     (12, 5), (12, 10), (13, 6), (13, 11), (14, 7), (14, 12),
                     (15, 15))
            for p in make_head_pieces(0):
                p()
            pp0 = make_prod_pieces(0)
            for _, j in PRODW:
                if pp0[j] is not None:
                    pp0[j]()
            for c in range(NCH):
                slots = [[] for _ in range(DTI)]
                if c >= 1:
                    tps = make_tail_pieces(c - 1)
                    for i in range(4):
                        slots[i].append(tps[i])
                if c + 1 < NCH:
                    hps = make_head_pieces(c + 1)
                    slots[4].append(hps[0])
                    slots[5].append(hps[1])
                    pps = make_prod_pieces(c + 1)
                    for w, j in PRODW:
                        if pps[j] is not None:
                            slots[w].append(pps[j])
                emit_scan(c, slots)
            for p in make_tail_pieces(NCH - 1):
                p()

            fence2 = ([b.now("V"), b.now("S"), b.now("T"), b.now("G"), b.now("CC"),
                       st["lastout"]] +
                      [b.now(k) for k in ("X1L0", "X1L1", "AGL0", "AGL1", "IP0", "IP1",
                                          "OP0", "OP1", "BCD", "BBG", "CBG",
                                          "OD0", "OD1", "XD0", "XD1")])

        with contextlib.ExitStack() as s3s:
            s3_x2 = s3s.enter_context(nc.sbuf_tensor("s3_x2", [128, 4, DM], F32))
            s3_tiles = []
            for rt4 in range(4):
                ev = b.on("g", ig(s3_x2[:, rt4, :], ag2out, s_ag2i[:, rt4:rt4 + 1]),
                          waits=fence2 + [ev_consts], sig="S3X")
                s3_tiles.append((s3_x2[:, rt4, :], ev))
            ev_moe3 = emit_moe("m3", s3_tiles, fence2, do_routing=False)
            ev_fin = b.on("y", dma(out_b0, outrows_d[0:TOKS, :]), waits=[ev_moe3], sig="FIN")
            b.on("y", lambda e: e.nop(), waits=[ev_fin, b.now("OD0"), b.now("OD1"), b.now("m3SC")])

        block.vector(lambda e: b.play("v", e, sems))
        block.scalar(lambda e: b.play("s", e, sems))
        block.tensor(lambda e: b.play("t", e, sems))
        block.gpsimd(lambda e: b.play("g", e, sems))
        block.sync(lambda e: b.play("y", e, sems))

    return nc


_CACHE = {}


def kernel(**inputs):
    x = np.ascontiguousarray(inputs["L0_attn_x"], dtype=np.float32)
    ei = np.ascontiguousarray(inputs["L0_ffn_expert_indices"], dtype=np.int32)
    ew = np.ascontiguousarray(inputs["L0_ffn_expert_weights"], dtype=np.float32)
    _ = inputs.get("L0_ffn_scores")

    if "nc" not in _CACHE:
        _CACHE["nc"] = build_program()
    nc = _CACHE["nc"]

    rep = {
        "ipwT": np.ascontiguousarray(inputs["L0_attn_in_proj_w"].T, dtype=np.float32),
        "opwT": np.ascontiguousarray(inputs["L0_attn_out_proj_w"].T, dtype=np.float32),
        "xpwT": np.ascontiguousarray(inputs["L0_attn_x_proj_w"].T, dtype=np.float32),
        "dtwT": np.ascontiguousarray(inputs["L0_attn_dt_proj_w"].T, dtype=np.float32),
        "convwT": np.ascontiguousarray(
            np.asarray(inputs["L0_attn_conv_w"]).reshape(DI, 4).reshape(DTI, 128, 4)
            .transpose(1, 0, 2).reshape(128, DTI * 4), dtype=np.float32),
        "convbT": np.ascontiguousarray(
            np.asarray(inputs["L0_attn_conv_b"]).reshape(DTI, 128).T, dtype=np.float32),
        "dtbT": np.ascontiguousarray(
            np.asarray(inputs["L0_attn_dt_proj_b"]).reshape(DTI, 128).T, dtype=np.float32),
        "dvecT": np.ascontiguousarray(
            np.asarray(inputs["L0_attn_D"]).reshape(DTI, 128).T, dtype=np.float32),
        "alogT": np.ascontiguousarray(
            np.asarray(inputs["L0_attn_A_log"]).reshape(DTI, 128, DS).transpose(1, 0, 2)
            .reshape(128, DTI * DS), dtype=np.float32),
        "w1": np.ascontiguousarray(inputs["L0_ffn_w1"], dtype=np.float32),
        "w2": np.ascontiguousarray(inputs["L0_ffn_w2"], dtype=np.float32),
        "bias": np.ascontiguousarray(inputs["L0_ffn_bias"], dtype=np.float32).reshape(1, -1),
        "tokT": np.ascontiguousarray(
            (np.arange(2 * TOKS) // 2).astype(np.float32).reshape(8, 128).T),
        "kofT": np.ascontiguousarray(
            (np.arange(2 * TOKS) % 2).astype(np.float32).reshape(8, 128).T),
        "iota256": np.arange(256, dtype=np.float32).reshape(1, -1),
        "eidf": np.arange(8, dtype=np.float32).reshape(8, 1),
        "identb": np.eye(128, dtype=BF),
        "identf": np.eye(128, dtype=np.float32),
    }
    in_maps = []
    for c in range(NCORE):
        t0 = c * TOKS
        ag2rows = np.zeros((128, 4), dtype=np.int32)
        for rt4 in range(4):
            ch = (c * TOKS + rt4 * 128) // TC
            off = (c * TOKS + rt4 * 128) % TC
            ag2rows[:, rt4] = ch * NCORE * TC + off + np.arange(128)
        m = dict(rep)
        m["myx"] = np.ascontiguousarray(x[c])
        m["x0s"] = np.ascontiguousarray(x[0, t0:t0 + TOKS])
        m["idxflat"] = np.ascontiguousarray(ei[t0:t0 + TOKS].reshape(1, -1))
        m["wT"] = np.ascontiguousarray(
            ew[t0:t0 + TOKS].reshape(-1).reshape(8, 128).T.astype(np.float32))
        m["flagcol"] = np.full((128, 1), 1.0 if c == 0 else 0.0, dtype=np.float32)
        m["ag2idx"] = ag2rows
        in_maps.append(m)

    res = run_bass_kernel_spmd(nc, in_maps, core_ids=list(range(NCORE))).results
    out = np.empty((NCORE, L, DM), dtype=np.float32)
    for c in range(NCORE):
        out[c] = res[c]["out_my"]
    for c in range(NCORE):
        out[0, c * TOKS:(c + 1) * TOKS] = res[c]["out_b0"]
    return out
